# version 03 (final)
# baseline (speedup 1.0000x reference)
import math
import numpy as np
import ml_dtypes
import concourse.bass as bass
import concourse.mybir as mybir
from concourse.bass_utils import run_bass_kernel_spmd

F32 = mybir.dt.float32
BF16 = mybir.dt.bfloat16
FP8 = mybir.dt.float8e4
I32 = mybir.dt.int32
AF = mybir.ActivationFunctionType
ALU = mybir.AluOpType
AX = mybir.AxisListType

T = 2048
D = 1024
NT = T // 128
NCH = D // 128
H = 16
DK = 64


INPUT_SPECS = [
    ("L0_attn_ln1_w", [1024]), ("L0_attn_mu_r", [1024]), ("L0_attn_mu_k", [1024]),
    ("L0_attn_mu_v", [1024]), ("L0_attn_mu_w", [1024]), ("L0_attn_mu_g", [1024]),
    ("L0_attn_Wr", [1024, 1024]), ("L0_attn_Wk", [1024, 1024]),
    ("L0_attn_Wv", [1024, 1024]), ("L0_attn_Wg", [1024, 1024]),
    ("L0_attn_Wo", [1024, 1024]), ("L0_attn_w0", [1024]),
    ("L0_attn_wA", [1024, 64]), ("L0_attn_wB", [64, 1024]),
    ("L0_attn_u", [16, 64]), ("L0_attn_gn_w", [16, 64]), ("L0_attn_gn_b", [16, 64]),
    ("L0_attn_ln2_w", [1024]), ("L0_attn_ffn_w1", [1024, 4096]),
    ("L0_attn_ffn_w2", [4096, 1024]),
    ("L0_ffn_Wm", [12288, 1024]), ("L0_ffn_Wp", [1024, 12288]),
    ("L0_ffn_Wc1", [256, 1024]), ("L0_ffn_Wc2", [12288, 256]),
    ("L1_attn_qkv_w", [3072, 1024]), ("L1_attn_qkv_b", [3072]),
    ("L1_attn_proj_w", [1024, 1024]), ("L1_attn_proj_b", [1024]),
    ("L1_attn_dwc_w", [1024, 1, 3]), ("L1_attn_dwc_b", [1024]),
    ("L1_ffn_m1", [1024, 64]), ("L1_ffn_m2", [64, 64, 64]), ("L1_ffn_mb", [64, 64]),
    ("L1_ffn_w1", [1024, 64, 64]), ("L1_ffn_w2", [64, 64, 1024]), ("L1_ffn_b2", [1024]),
    ("L2_attn_Wq", [1024, 1024]), ("L2_attn_Wk", [1024, 1024]),
    ("L2_attn_Wv", [1024, 1024]), ("L2_attn_Wo", [1024, 1024]),
    ("L2_attn_bo", [1024]), ("L2_attn_rel_pos_bias", [16, 1024]),
    ("L2_ffn_gate_w", [8, 1024]), ("L2_ffn_w_gate", [8, 4096, 1024]),
    ("L2_ffn_w_up", [8, 4096, 1024]), ("L2_ffn_w_down", [8, 1024, 4096]),
    ("L3_attn_Wq", [1024, 1024]), ("L3_attn_bq", [1024]),
    ("L3_attn_Wk", [1024, 1024]), ("L3_attn_bk", [1024]),
    ("L3_attn_Wv", [1024, 1024]), ("L3_attn_bv", [1024]),
    ("L3_attn_Wo", [1024, 1024]), ("L3_attn_bo", [1024]),
    ("L3_ffn_weight", [1024, 1024]), ("L3_ffn_bias", [1024]),
]

ENGINES = ("sync", "tensor", "vector", "scalar", "gpsimd")
AUTO = ("vector", "scalar", "gpsimd")
META = {"wait_ge", "nop", "sem_inc", "wait_op"}
NOSERIAL = {"dma_start", "dma_start_transpose", "indirect_dma_start",
            "collective_compute", "dma_gather", "dma_scatter_add"}


class Sem:
    __slots__ = ("h", "n")

    def __init__(self, h):
        self.h = h
        self.n = 0


class RecInstr:
    __slots__ = ("name", "args", "kwargs", "incs", "_dma_rec")

    def __init__(self, name, args, kwargs):
        self.name = name
        self.args = args
        self.kwargs = kwargs
        self.incs = []
        self._dma_rec = None

    def then_inc(self, sem, k):
        if self._dma_rec is not None and sem.n > 0:
            rec = self._dma_rec
            i = rec.ops.index(self)
            rec.ops.insert(i, RecInstr("wait_ge", (sem, sem.n), {}))
            self._dma_rec = None
        sem.n += k
        self.incs.append((sem, k))
        return self


class Rec:
    def __init__(self, name):
        self.name = name
        self.ops = []
        self.serial = None

    def op(self, name, *args, **kwargs):
        auto = (self.serial is not None and name not in META
                and name not in NOSERIAL)
        if auto and self.serial.n > 0:
            self.ops.append(RecInstr("wait_ge", (self.serial, self.serial.n), {}))
        r = RecInstr(name, args, kwargs)
        if name in ("dma_start", "dma_start_transpose", "indirect_dma_start"):
            r._dma_rec = self
        self.ops.append(r)
        if auto:
            r.then_inc(self.serial, 1)
        return r

    def __getattr__(self, name):
        if name.startswith("_"):
            raise AttributeError(name)

        def call(*args, **kwargs):
            return self.op(name, *args, **kwargs)
        return call

    def wait(self, sem, val=None):
        self.op("wait_ge", sem, sem.n if val is None else val)

    def wait_tok(self, tok):
        sem, val = tok
        if val > 0:
            self.op("wait_ge", sem, val)

    def replay(self, eng):
        for r in self.ops:
            assert len(r.incs) <= 1, f"{r.name} carries {len(r.incs)} sem updates"
            args = [a.h if isinstance(a, Sem) else a for a in r.args]
            kwargs = {k: (v.h if isinstance(v, Sem) else v)
                      for k, v in r.kwargs.items()}
            inst = getattr(eng, r.name)(*args, **kwargs)
            for sem, k in r.incs:
                inst.then_inc(sem.h, k)


class KS:
    def __init__(self, nc, es):
        self.nc = nc
        self._es = es
        self.r = {e: Rec(e) for e in ENGINES}
        self.dram = {}
        self.sb = {}
        self._pool = []
        self._stage = []
        self._nalloc = 0
        self.bar = self.gsem()
        for e in AUTO:
            self.r[e].serial = self.gsem()

    def gsem(self):
        h = self._es.enter_context(self.nc.semaphore(f"gs{self._nalloc}"))
        self._nalloc += 1
        return Sem(h)

    def swsem(self):
        return self.gsem()

    def sem(self):
        if self._pool:
            s = self._pool.pop()
        else:
            s = self.gsem()
        self._stage.append(s)
        return s

    def tok(self, e):
        s = self.r[e].serial
        return (s, s.n)

    def dram_scratch(self, name, shape, dtype):
        t = self.nc.dram_tensor(name, shape, dtype, kind="Internal")
        self.dram[name] = t.ap()
        return self.dram[name]

    def sbuf(self, name, shape, dtype):
        t = self._es.enter_context(self.nc.sbuf_tensor(name, shape, dtype))
        self.sb[name] = t.ap()
        return self.sb[name]

    def psum(self, name, shape, dtype=F32):
        t = self._es.enter_context(self.nc.psum_tensor(name, shape, dtype))
        self.sb[name] = t.ap()
        return self.sb[name]

    def barrier(self):
        b = self.bar
        for e in AUTO:
            r = self.r[e]
            if r.serial.n > 0:
                r.op("wait_ge", r.serial, r.serial.n)
        target = b.n + len(ENGINES)
        for e in ENGINES:
            self.r[e].op("nop").then_inc(b, 1)
        assert b.n == target
        for e in ENGINES:
            self.r[e].wait(b, target)
        self._pool.extend(self._stage)
        self._stage = []


def tmv(ap, tt, cols=D):
    return ap[:, tt * cols:(tt + 1) * cols]


def bcast_dram_row(tensor_ap, cols, parts=128):
    return bass.AP(tensor_ap.tensor, 0, [[0, parts], [1, cols]])


def tm_dram(dram_ap):
    return dram_ap[:, :].rearrange("(n p) d -> p n d", p=128)


def a3d(ap, cols=D):
    return ap.rearrange("p (n d) -> p n d", d=cols)


def st_ln(ks, xsrc, xn_dram, xnT_dram):
    v, s, t, sy, g = (ks.r[e] for e in ("vector", "scalar", "tensor", "sync", "gpsimd"))
    A1, B1, B2, S1 = ks.sb["A1"], ks.sb["B1"], ks.sb["B2"], ks.sb["S1"]
    PT = ks.sb["PT"]
    ident = ks.sb["ident"]
    ld, st = ks.sem(), ks.sem()

    sy.dma_start(out=a3d(A1), in_=tm_dram(xsrc)).then_inc(ld, 16)
    v.wait(ld)
    a3 = A1.rearrange("p (n d) -> p n d", d=D)
    v.tensor_reduce(S1[:, 0:NT], a3, AX.X, ALU.add)
    Bf = B1.bitcast(F32)
    for half in range(2):
        v.tensor_mul(Bf[:, :], A1[:, half * 8192:(half + 1) * 8192],
                     A1[:, half * 8192:(half + 1) * 8192])
        v.tensor_reduce(S1[:, 16 + half * 8:16 + (half + 1) * 8],
                        Bf.rearrange("p (n d) -> p n d", d=D), AX.X, ALU.add)
    v.tensor_scalar(S1[:, 0:32], S1[:, 0:32], 1.0 / D, None, ALU.mult)
    v.tensor_mul(S1[:, 32:48], S1[:, 0:16], S1[:, 0:16])
    v.tensor_sub(S1[:, 32:48], S1[:, 16:32], S1[:, 32:48])
    v.tensor_scalar(S1[:, 32:48], S1[:, 32:48], 1e-5, None, ALU.add)
    s.wait_tok(ks.tok("vector"))
    s.sqrt(S1[:, 48:64], S1[:, 32:48])
    v.wait_tok(ks.tok("scalar"))
    v.reciprocal(S1[:, 32:48], S1[:, 48:64])
    mean_b = S1[:, 0:16].unsqueeze(2).broadcast_to([128, NT, D])
    rstd_b = S1[:, 32:48].unsqueeze(2).broadcast_to([128, NT, D])
    v.tensor_tensor(a3, a3, mean_b, ALU.subtract)
    v.tensor_tensor(a3, a3, rstd_b, ALU.mult)
    xn_tok = ks.tok("vector")
    v.tensor_copy(B1[:, :], A1[:, :])
    b1_tok = ks.tok("vector")
    s.wait_tok(xn_tok)
    s.dma_start(out=tm_dram(xn_dram), in_=a3d(A1)).then_inc(st, 16)
    s.wait_tok(b1_tok)
    xnbf = ks.dram["xn_bf"]
    s.dma_start(out=xnbf[0:T, :].rearrange("(n p) d -> p n d", p=128),
                in_=a3d(B1)).then_inc(st, 16)
    t.wait_tok(b1_tok)
    cp2 = ks.sem()
    prev_copy = None
    for tt in range(NT):
        for kc in range(NCH):
            if prev_copy is not None:
                t.wait_tok(prev_copy)
            t.transpose(PT[:, :], B1[:, tt * D + kc * 128: tt * D + (kc + 1) * 128],
                        ident[:, :]).then_inc(cp2, 1)
            v.wait(cp2)
            v.tensor_copy(B2[:, (tt * NCH + kc) * 128:(tt * NCH + kc + 1) * 128],
                          PT[:, :])
            prev_copy = ks.tok("vector")
        s.wait_tok(prev_copy)
        dst = bass.AP(xnT_dram.tensor, tt * 128,
                      [[T, 128], [128 * T, NCH], [1, 128]])
        s.dma_start(out=dst,
                    in_=B2[:, tt * NCH * 128:(tt + 1) * NCH * 128]
                    .rearrange("p (c j) -> p c j", j=128)).then_inc(st, 16)
    s.wait(st)


def load_xnT(ks, xnT_dram, dest, sem):
    src = bass.AP(xnT_dram.tensor, 0, [[T, 128], [128 * T, NCH], [1, T]])
    ks.r["sync"].dma_start(out=dest.rearrange("p (c t) -> p c t", t=T),
                           in_=src).then_inc(sem, 16)


def transpose_tm_to_fm(ks, src_bf, dst_bf, ntt=NT, nch=NCH):
    t, v = ks.r["tensor"], ks.r["vector"]
    PT = ks.sb["PT"]
    ident = ks.sb["ident"]
    cp2 = ks.sem()
    prev_copy = None
    for tt in range(ntt):
        for kc in range(nch):
            if prev_copy is not None:
                t.wait_tok(prev_copy)
            t.transpose(PT[:, :], src_bf[:, tt * (nch * 128) + kc * 128:
                                         tt * (nch * 128) + (kc + 1) * 128],
                        ident[:, :]).then_inc(cp2, 1)
            v.wait(cp2)
            v.tensor_copy(dst_bf[:, (kc * ntt + tt) * 128:(kc * ntt + tt + 1) * 128],
                          PT[:, :])
            prev_copy = ks.tok("vector")
    return prev_copy


class MM:

    def __init__(self, ks):
        self.ks = ks
        self.freecnt = [None, None, None, None]
        self.PS = [ks.sb["PS0"], ks.sb["PS1"], ks.sb["PS2"], ks.sb["PS3"]]

    def run(self, actT_fn, w_fn, evac_fn, n_oc, ntt=NT, nkc=NCH):
        ks = self.ks
        t = ks.r["tensor"]
        mmv = ks.sem()
        for tt in range(ntt):
            for oc in range(n_oc):
                b = (tt * n_oc + oc) % 4
                ps = self.PS[b]
                if self.freecnt[b] is not None:
                    t.wait_tok(self.freecnt[b])
                for kc in range(nkc):
                    mi = t.matmul(ps[:, :], actT_fn(kc, tt), w_fn(kc, oc),
                                  start=(kc == 0), stop=(kc == nkc - 1))
                mi.then_inc(mmv, 1)
                evac_fn(tt, oc, ps, mmv)
                self.freecnt[b] = ks.tok("vector")
        return mmv



def load_w_bf16(ks, w_dram, dest_bf, ncols=D, nkc=NCH):
    sy, v = ks.r["sync"], ks.r["vector"]
    WST = ks.sb["WST"]
    toks = [ks.tok("vector"), ks.tok("vector")]
    sems = [ks.sem(), ks.sem()]
    si = 0
    for kc in range(nkc):
        for c0 in range(0, ncols, 1024):
            ch = min(1024, ncols - c0)
            sl = si % 2
            sy.wait_tok(toks[sl])
            sy.dma_start(out=WST[:, sl * 1024:sl * 1024 + ch],
                         in_=w_dram[kc * 128:(kc + 1) * 128, c0:c0 + ch]).then_inc(sems[sl], 16)
            v.wait(sems[sl])
            v.tensor_copy(dest_bf[:, kc * ncols + c0:kc * ncols + c0 + ch],
                          WST[:, sl * 1024:sl * 1024 + ch])
            toks[sl] = ks.tok("vector")
            si += 1
    return ks.tok("vector")


def st_l3_attn(ks, W, xnT_dram, s_dram):
    v, s, t, sy, g = (ks.r[e] for e in ("vector", "scalar", "tensor", "sync", "gpsimd"))
    A1, B1, B2, C1, S1, S2, CB = (ks.sb[k] for k in ("A1", "B1", "B2", "C1", "S1", "S2", "CB"))
    A1b = A1.bitcast(BF16)
    C1b = C1.bitcast(BF16)
    PR, PQ, PE1 = ks.sb["PR"], ks.sb["PQ"], ks.sb["PE1"]
    onesb, onesf = ks.sb["onesb"], ks.sb["onesf"]
    cos_d, sin_d = ks.dram["rope_cos"], ks.dram["rope_sin"]
    qtm_d, ktm_d, vtm_d = ks.dram["scr_q"], ks.dram["scr_k"], ks.dram["scr_v"]
    ld = ks.sem()
    mm = MM(ks)

    load_xnT(ks, xnT_dram, B1, ld)
    for i, bn in enumerate(("bq", "bk", "bv")):
        sy.dma_start(out=C1[:, i * D:(i + 1) * D],
                     in_=bcast_dram_row(W[bn], D)).then_inc(ld, 16)

    state = {"prev_mm": None, "prev_store": None}

    def one_proj(wname, bias_i, out_dram, rope):
        stp = ks.sem()
        if state["prev_mm"] is not None:
            v.wait(*state["prev_mm"])
        wtok = load_w_bf16(ks, W[wname], CB)
        t.wait(ld)
        t.wait_tok(wtok)
        v.wait(ld)

        def actT(kc, tt):
            return B1[:, kc * T + tt * 128:kc * T + (tt + 1) * 128]

        def wfn(kc, oc):
            return CB[:, kc * D + oc * 512:kc * D + (oc + 1) * 512]

        def evac(tt, oc, ps, mmv):
            v.wait(mmv)
            v.scalar_tensor_tensor(
                tmv(A1, tt)[:, oc * 512:(oc + 1) * 512], ps[:, :], 1.0,
                C1[:, bias_i * D + oc * 512:bias_i * D + (oc + 1) * 512],
                ALU.mult, ALU.add)

        if state["prev_store"] is not None:
            v.wait(state["prev_store"])
        mmv = mm.run(actT, wfn, evac, 2)
        rope_toks = []
        for tt in range(NT):
            if rope:
                csl = ks.sem()
                half = (tt % 2) * 1024
                if tt >= 2:
                    sy.wait_tok(rope_toks[tt - 2])
                elif state.get("last_rope"):
                    sy.wait_tok(state["last_rope"])
                sy.dma_start(out=S1[:, half:half + 512],
                             in_=bass.AP(cos_d.tensor, tt * 128 * 32,
                                         [[32, 128], [0, 16], [1, 32]])).then_inc(csl, 16)
                sy.dma_start(out=S1[:, half + 512:half + 1024],
                             in_=bass.AP(sin_d.tensor, tt * 128 * 32,
                                         [[32, 128], [0, 16], [1, 32]])).then_inc(csl, 16)
                a = tmv(A1, tt)
                x1 = a.rearrange("p (m two) -> p m two", two=2)[:, :, 0:1].squeeze(2)
                x2 = a.rearrange("p (m two) -> p m two", two=2)[:, :, 1:2].squeeze(2)
                dst = tmv(B2, tt)
                o1 = dst.rearrange("p (m two) -> p m two", two=2)[:, :, 0:1].squeeze(2)
                o2 = dst.rearrange("p (m two) -> p m two", two=2)[:, :, 1:2].squeeze(2)
                co, si = S1[:, half:half + 512], S1[:, half + 512:half + 1024]
                t1, t2 = S2[:, 0:512], S2[:, 512:1024]
                v.wait(csl)
                v.tensor_tensor(t1, x1, co, ALU.mult)
                v.tensor_tensor(t2, x2, si, ALU.mult)
                v.tensor_tensor(o1, t1, t2, ALU.subtract)
                v.tensor_tensor(t1, x1, si, ALU.mult)
                v.tensor_tensor(t2, x2, co, ALU.mult)
                v.tensor_tensor(o2, t1, t2, ALU.add)
            else:
                v.tensor_copy(tmv(B2, tt), tmv(A1, tt))
            rope_toks.append(ks.tok("vector"))
        s.wait_tok(rope_toks[-1])
        s.dma_start(out=tm_dram(out_dram), in_=a3d(B2)).then_inc(stp, 16)
        state["prev_mm"] = (mmv, mmv.n)
        state["prev_store"] = stp
        state["last_rope"] = rope_toks[-1]
        return stp

    one_proj("Wq", 0, qtm_d, True)
    one_proj("Wk", 1, ktm_d, True)
    stv = one_proj("Wv", 2, vtm_d, False)
    s.wait(stv)
    ks.barrier()

    ldq, ldk, ldv = ks.sem(), ks.sem(), ks.sem()
    sy.dma_start(out=a3d(A1b[:, 0:16384]), in_=tm_dram(qtm_d)).then_inc(ldq, 16)
    v.wait(ldq)
    t.wait(ldq)
    tk = transpose_tm_to_fm(ks, A1b, B1)
    sy.wait_tok(tk)
    sy.dma_start(out=a3d(A1b[:, 0:16384]), in_=tm_dram(ktm_d)).then_inc(ldk, 16)
    v.wait(ldk)
    t.wait(ldk)
    tk = transpose_tm_to_fm(ks, A1b, B2)
    sy.wait_tok(tk)
    sy.dma_start(out=a3d(C1b), in_=tm_dram(vtm_d)).then_inc(ldv, 16)
    ks.barrier()

    sT = A1b[:, 16384:32768]
    mm2 = ks.sem()
    t.wait(ldv)
    gate_tok = None
    for h in range(H):
        kc = h // 2
        pb = (h % 2) * 64
        for ic in range(4):
            for jc in range(NT):
                if gate_tok is not None:
                    t.wait_tok(gate_tok)
                t.matmul(PQ[:, :], B2[pb:pb + 64, kc * T + jc * 128:kc * T + (jc + 1) * 128],
                         B1[pb:pb + 64, kc * T + ic * 512:kc * T + (ic + 1) * 512],
                         start=True, stop=True).then_inc(mm2, 1)
                s.wait(mm2)
                s.activation(A1b[:, jc * 512:(jc + 1) * 512], PQ[:, :], AF.Exp,
                             scale=0.125)
                gate_tok = ks.tok("scalar")
            t.wait_tok(gate_tok)
            for jc in range(NT):
                mi1 = t.matmul(PR[pb:pb + 64, :], C1b[:, jc * D + h * 64:jc * D + (h + 1) * 64],
                               A1b[:, jc * 512:(jc + 1) * 512],
                               start=(jc == 0), stop=(jc == NT - 1))
                mi2 = t.matmul(PE1[pb:pb + 1, :], onesb[:, 0:1],
                               A1b[:, jc * 512:(jc + 1) * 512],
                               start=(jc == 0), stop=(jc == NT - 1))
            mi1.then_inc(mm2, 1)
            mi2.then_inc(mm2, 1)
            v.wait(mm2)
            v.reciprocal(S1[pb:pb + 1, 0:512], PE1[pb:pb + 1, :])
            t.wait_tok(ks.tok("vector"))
            t.matmul(PE1[pb:pb + 64, :], onesf[pb:pb + 1, 0:64],
                     S1[pb:pb + 1, 0:512], start=True, stop=True).then_inc(mm2, 1)
            s.wait(mm2)
            s.copy(S2[pb:pb + 64, 0:512], PE1[pb:pb + 64, :])
            v.wait_tok(ks.tok("scalar"))
            v.tensor_tensor(sT[pb:pb + 64, kc * T + ic * 512:kc * T + (ic + 1) * 512],
                            PR[pb:pb + 64, :], S2[pb:pb + 64, 0:512], ALU.mult)
            gate_tok = ks.tok("vector")
    ks.barrier()

    ld3 = ks.sem()
    wtok = load_w_bf16(ks, W["Wo"], CB)
    sy.dma_start(out=C1[:, 4 * D:5 * D], in_=bcast_dram_row(W["bo"], D)).then_inc(ld3, 16)
    t.wait(ld3)
    t.wait_tok(wtok)
    v.wait(ld3)

    def actT3(kc, tt):
        return sT[:, kc * T + tt * 128:kc * T + (tt + 1) * 128]

    def wfn3(kc, oc):
        return CB[:, kc * D + oc * 512:kc * D + (oc + 1) * 512]

    slot_sems = [None, None, None, None]

    def evac3(tt, oc, ps, mmv):
        if oc == 0 and slot_sems[tt % 4] is not None:
            v.wait(slot_sems[tt % 4])
        v.wait(mmv)
        v.scalar_tensor_tensor(
            tmv(C1, tt % 4)[:, oc * 512:(oc + 1) * 512], ps[:, :], 1.0,
            C1[:, 4 * D + oc * 512:4 * D + (oc + 1) * 512],
            ALU.mult, ALU.add)
        if oc == 1:
            stf = ks.sem()
            s.wait_tok(ks.tok("vector"))
            s.dma_start(out=tm_dram(s_dram)[:, tt, :],
                        in_=tmv(C1, tt % 4)).then_inc(stf, 16)
            slot_sems[tt % 4] = stf

    mm.run(actT3, wfn3, evac3, 2)
    for ssem in slot_sems:
        s.wait(ssem)



def mm_fm(ks, w_cb_fn, act_fn, evac_fn, n_oc=8, n_tc=4, nkc=NCH):
    t = ks.r["tensor"]
    PS = [ks.sb["PS0"], ks.sb["PS1"], ks.sb["PS2"], ks.sb["PS3"]]
    mmv = ks.sem()
    freetoks = [None] * 4
    for ofc in range(n_oc):
        for kc in range(nkc):
            for tc in range(n_tc):
                if kc == 0 and freetoks[tc] is not None:
                    t.wait_tok(freetoks[tc])
                mi = t.matmul(PS[tc][:, :], w_cb_fn(kc, ofc), act_fn(kc, tc),
                              start=(kc == 0), stop=(kc == nkc - 1))
                if kc == nkc - 1:
                    mi.then_inc(mmv, 1)
        for tc in range(n_tc):
            evac_fn(ofc, tc, PS[tc], mmv)
            freetoks[tc] = ks.tok("vector")
    return mmv


def st_l2_attn(ks, W, xnT_dram, s_dram):
    v, s, t, sy, g = (ks.r[e] for e in ("vector", "scalar", "tensor", "sync", "gpsimd"))
    A1, B1, B2, C1, S1, S2, CB = (ks.sb[k] for k in ("A1", "B1", "B2", "C1", "S1", "S2", "CB"))
    EXT = ks.sb["EXT"]
    A1b = A1.bitcast(BF16)
    C1b = C1.bitcast(BF16)
    PR, PQ, PE1 = ks.sb["PR"], ks.sb["PQ"], ks.sb["PE1"]
    onesb, onesf = ks.sb["onesb"], ks.sb["onesf"]
    pbf_d = ks.dram["pbfwd"]
    ld = ks.sem()
    mm = MM(ks)

    ldr = ks.sem()
    strow = ks.sem()
    sy.dma_start(out=S1[0:16, 0:1024], in_=W["rel_pos_bias"][:, :]).then_inc(ldr, 16)
    v.wait(ldr)
    rowb = S2.bitcast(BF16)
    v.memset(rowb[0:16, 0:2047], -15625000.0)
    rev = bass.AP(S1.tensor, 1023, [[2048, 16], [-1, 1024]])
    v.tensor_scalar(rowb[0:16, 2047:3071], rev, 1.0 / 64.0, None, ALU.mult)
    v.tensor_scalar(rowb[0:16, 3071:4096],
                    S1[0:16, 0:1].broadcast_to([16, 1025]), 1.0 / 64.0, None, ALU.mult)
    s.wait_tok(ks.tok("vector"))
    s.dma_start(out=pbf_d[:, :], in_=rowb[0:16, 0:4096]).then_inc(strow, 16)

    load_xnT(ks, xnT_dram, B1, ld)
    t.wait(ld)
    v.wait(ld)

    prev_mm = {"s": None}

    def qkv_fm(wname, dest_bf):
        if prev_mm["s"] is not None:
            v.wait(*prev_mm["s"])
        wtok = load_w_bf16(ks, W[wname], CB)
        t.wait_tok(wtok)

        def wfn(kc, ofc):
            return CB[:, kc * D + ofc * 128:kc * D + (ofc + 1) * 128]

        def actfn(kc, tc):
            return B1[:, kc * T + tc * 512:kc * T + (tc + 1) * 512]

        def evac(ofc, tc, ps, mmv):
            v.wait(mmv)
            v.tensor_copy(dest_bf[:, ofc * T + tc * 512:ofc * T + (tc + 1) * 512],
                          ps[:, :])

        mv = mm_fm(ks, wfn, actfn, evac)
        prev_mm["s"] = (mv, mv.n)

    qkv_fm("Wq", B2)
    qkv_fm("Wk", A1b[:, 0:16384])
    v.wait(*prev_mm["s"])
    wtok = load_w_bf16(ks, W["Wv"], CB)
    t.wait_tok(wtok)

    def actT(kc, tt):
        return B1[:, kc * T + tt * 128:kc * T + (tt + 1) * 128]

    def wfnv(kc, oc):
        return CB[:, kc * D + oc * 512:kc * D + (oc + 1) * 512]

    def evacv(tt, oc, ps, mmv):
        v.wait(mmv)
        v.tensor_copy(tmv(C1b, tt)[:, oc * 512:(oc + 1) * 512], ps[:, :])

    mm.run(actT, wfnv, evacv, 2)
    ks.barrier()

    sT = A1b[:, 16384:32768]
    mm2 = ks.sem()
    rowsems = [ks.sem() for _ in range(8)]
    prev_head_done = None
    for h in range(H):
        kc = h // 2
        pb = (h % 2) * 64
        if h == 0:
            sy.wait(strow)
        if prev_head_done is not None:
            for rs in rowsems:
                sy.wait_tok(prev_head_done)
                break
        for p in range(128):
            rs = rowsems[p % 8]
            sy.dma_start(out=EXT[p:p + 1, 0:2432],
                         in_=bass.AP(pbf_d.tensor, h * 4096 + 1663 - p,
                                     [[1, 1], [1, 2432]])).then_inc(rs, 16)
        for rs in rowsems:
            t.wait(rs)
            v.wait(rs)
        gate_tok = None
        for ic in range(4):
            njc = 4 * ic + 4
            for jc in range(njc):
                if gate_tok is not None:
                    t.wait_tok(gate_tok)
                t.matmul(PQ[:, :], A1b[pb:pb + 64, kc * T + jc * 128:kc * T + (jc + 1) * 128],
                         B2[pb:pb + 64, kc * T + ic * 512:kc * T + (ic + 1) * 512],
                         start=True, stop=True).then_inc(mm2, 1)
                v.wait(mm2)
                base = ic * 512 - jc * 128 + 384
                v.scalar_tensor_tensor(S2[:, 0:512], PQ[:, :], 1.0 / 64.0,
                                       EXT[:, base:base + 512], ALU.mult, ALU.add)
                s.wait_tok(ks.tok("vector"))
                s.activation(CB[:, jc * 512:(jc + 1) * 512],
                             S2[:, 0:512], AF.Exp)
                gate_tok = ks.tok("scalar")
            t.wait_tok(gate_tok)
            Pb = CB
            for jc in range(njc):
                mi1 = t.matmul(PR[pb:pb + 64, :], C1b[:, jc * D + h * 64:jc * D + (h + 1) * 64],
                               Pb[:, jc * 512:(jc + 1) * 512],
                               start=(jc == 0), stop=(jc == njc - 1))
                mi2 = t.matmul(PE1[pb:pb + 1, :], onesb[:, 0:1],
                               Pb[:, jc * 512:(jc + 1) * 512],
                               start=(jc == 0), stop=(jc == njc - 1))
            mi1.then_inc(mm2, 1)
            mi2.then_inc(mm2, 1)
            v.wait(mm2)
            v.reciprocal(S1[pb:pb + 1, 0:512], PE1[pb:pb + 1, :])
            t.wait_tok(ks.tok("vector"))
            t.matmul(PE1[pb:pb + 64, :], onesf[pb:pb + 1, 0:64],
                     S1[pb:pb + 1, 0:512], start=True, stop=True).then_inc(mm2, 1)
            s.wait(mm2)
            s.copy(S2[pb:pb + 64, 0:512], PE1[pb:pb + 64, :])
            v.wait_tok(ks.tok("scalar"))
            v.tensor_tensor(sT[pb:pb + 64, kc * T + ic * 512:kc * T + (ic + 1) * 512],
                            PR[pb:pb + 64, :], S2[pb:pb + 64, 0:512], ALU.mult)
            gate_tok = ks.tok("vector")
        prev_head_done = gate_tok
    ks.barrier()

    ld3 = ks.sem()
    wtok = load_w_bf16(ks, W["Wo"], CB)
    sy.dma_start(out=C1[:, 4 * D:5 * D], in_=bcast_dram_row(W["bo"], D)).then_inc(ld3, 16)
    t.wait(ld3)
    t.wait_tok(wtok)
    v.wait(ld3)

    def actT3(kc, tt):
        return sT[:, kc * T + tt * 128:kc * T + (tt + 1) * 128]

    def wfn3(kc, oc):
        return CB[:, kc * D + oc * 512:kc * D + (oc + 1) * 512]

    slot_sems = [None, None, None, None]

    def evac3(tt, oc, ps, mmv):
        if oc == 0 and slot_sems[tt % 4] is not None:
            v.wait(slot_sems[tt % 4])
        v.wait(mmv)
        v.scalar_tensor_tensor(
            tmv(C1, tt % 4)[:, oc * 512:(oc + 1) * 512], ps[:, :], 1.0,
            C1[:, 4 * D + oc * 512:4 * D + (oc + 1) * 512],
            ALU.mult, ALU.add)
        if oc == 1:
            stf = ks.sem()
            s.wait_tok(ks.tok("vector"))
            s.dma_start(out=tm_dram(s_dram)[:, tt, :],
                        in_=tmv(C1, tt % 4)).then_inc(stf, 16)
            slot_sems[tt % 4] = stf

    mm.run(actT3, wfn3, evac3, 2)
    for ssem in slot_sems:
        s.wait(ssem)


def st_l3_ffn(ks, W, xn_dram, f_dram):
    v, s, t, sy, g = (ks.r[e] for e in ("vector", "scalar", "tensor", "sync", "gpsimd"))
    A1, B1, B2, C1, S1, S2, CB = (ks.sb[k] for k in ("A1", "B1", "B2", "C1", "S1", "S2", "CB"))
    MI = ks.sb["MI"]
    QF8 = MI.bitcast(FP8)
    PT = ks.sb["PT"]
    ident = ks.sb["ident"]
    ld = ks.sem()

    def quantize(src_f32, nt_tiles, sc_base):
        a4 = src_f32.rearrange("p (n w) -> p n w", w=32)
        nblk = nt_tiles * 32
        sc = C1[:, sc_base:sc_base + nblk]
        rsc = C1[:, sc_base + 512:sc_base + 512 + nblk]
        v.tensor_reduce(sc, a4, AX.X, ALU.max, apply_absolute_value=True)
        v.tensor_scalar(sc, sc, 1.0 / 7.5, 1e-30, ALU.mult, ALU.max)
        v.reciprocal(rsc, sc)
        rscb = rsc.unsqueeze(2).broadcast_to([128, nblk, 32])
        v.tensor_tensor(a4, a4, rscb, ALU.mult)
        y_tok = ks.tok("vector")
        sel_tok = None
        for tt in range(nt_tiles):
            y = src_f32[:, tt * D:(tt + 1) * D]
            aab = S2[:, 0:1024]
            m1 = S1[:, 0:1024]
            tq = S2[:, 1024:2048]
            mski = S1[:, 1024:2048].bitcast(I32)
            s.wait_tok(y_tok)
            if sel_tok is not None:
                s.wait_tok(sel_tok)
            s.activation(aab, y, AF.Abs)
            abs_tok = ks.tok("scalar")
            v.tensor_scalar(m1, y, 8.0, None, ALU.mult)
            v.tensor_scalar(m1, m1, 8388608.0, None, ALU.add)
            v.tensor_scalar(m1, m1, 8388608.0, None, ALU.subtract)
            v.tensor_scalar(m1, m1, 0.125, None, ALU.mult)
            v.tensor_copy(QF8[:, :], y)
            v.tensor_copy(tq, QF8[:, :])
            v.wait_tok(abs_tok)
            v.tensor_scalar(mski, aab, 1.0, None, ALU.is_lt)
            v.select(tq, mski, m1, tq, add_drain=True)
            sel_tok = ks.tok("vector")
            scb = C1[:, sc_base + tt * 32:sc_base + (tt + 1) * 32]\
                .unsqueeze(2).broadcast_to([128, 32, 32])
            v.tensor_tensor(tmv(B2, tt).rearrange("p (n w) -> p n w", w=32),
                            tq.rearrange("p (n w) -> p n w", w=32), scb, ALU.mult)

    sy.dma_start(out=a3d(A1), in_=tm_dram(xn_dram)).then_inc(ld, 16)
    v.wait(ld)
    quantize(A1[:, :], NT, 0)
    t.wait_tok(ks.tok("vector"))
    xdT_tok = transpose_tm_to_fm(ks, B2, B1)
    v.wait_tok(xdT_tok)
    sy.wait_tok(xdT_tok)
    sy.dma_start(out=a3d(A1[:, 0:8 * D]),
                 in_=W["weight"][:, :].rearrange("(n p) d -> p n d", p=128)).then_inc(ld, 16)
    v.wait(ld)
    quantize(A1[:, 0:8 * D], NCH, 1536)
    t.wait_tok(ks.tok("vector"))
    cpb = ks.sem()
    prev_copy = None
    for ot in range(NCH):
        for kc in range(NCH):
            if prev_copy is not None:
                t.wait_tok(prev_copy)
            t.transpose(PT[:, :], B2[:, ot * D + kc * 128:ot * D + (kc + 1) * 128],
                        ident[:, :]).then_inc(cpb, 1)
            v.wait(cpb)
            v.tensor_copy(CB[:, kc * D + ot * 128:kc * D + (ot + 1) * 128],
                          PT[:, :])
            prev_copy = ks.tok("vector")
    sy.dma_start(out=C1[:, 0:D], in_=bcast_dram_row(W["bias"], D)).then_inc(ld, 16)
    v.wait(ld)
    t.wait_tok(prev_copy)
    mm = MM(ks)

    def actT(kc, tt):
        return B1[:, kc * T + tt * 128:kc * T + (tt + 1) * 128]

    def wfn(kc, oc):
        return CB[:, kc * D + oc * 512:kc * D + (oc + 1) * 512]

    slot_sems = [None, None, None, None]

    def evac(tt, oc, ps, mmv):
        if oc == 0 and slot_sems[tt % 4] is not None:
            v.wait(slot_sems[tt % 4])
        v.wait(mmv)
        v.scalar_tensor_tensor(
            tmv(C1, 1 + tt % 4)[:, oc * 512:(oc + 1) * 512], ps[:, :], 1.0,
            C1[:, oc * 512:(oc + 1) * 512], ALU.mult, ALU.add)
        if oc == 1:
            stf = ks.sem()
            s.wait_tok(ks.tok("vector"))
            s.dma_start(out=tm_dram(f_dram)[:, tt, :],
                        in_=tmv(C1, 1 + tt % 4)).then_inc(stf, 16)
            slot_sems[tt % 4] = stf

    mm.run(actT, wfn, evac, 2)
    for ssem in slot_sems:
        s.wait(ssem)



def conv_dram_bf16(ks, src_f32, dst_bf16, rows, cols):
    sy, s, v = ks.r["sync"], ks.r["scalar"], ks.r["gpsimd"]
    vv = ks.r["vector"]
    WST, S2 = ks.sb["WST"], ks.sb["S2"]
    S2b = S2.bitcast(BF16)
    sems = [ks.sem(), ks.sem()]
    osems = [ks.sem(), ks.sem()]
    toks = [ks.tok("vector"), ks.tok("vector")]
    for ch in range(rows // 128):
        sl = ch % 2
        sy.wait_tok(toks[sl])
        sy.dma_start(out=WST[:, sl * 1024:(sl + 1) * 1024],
                     in_=src_f32[ch * 128:(ch + 1) * 128, :]).then_inc(sems[sl], 16)
        vv.wait(sems[sl])
        vv.wait(osems[sl])
        vv.tensor_copy(S2b[:, sl * 1024:(sl + 1) * 1024],
                       WST[:, sl * 1024:(sl + 1) * 1024])
        toks[sl] = ks.tok("vector")
        s.wait_tok(toks[sl])
        s.dma_start(out=dst_bf16[ch * 128:(ch + 1) * 128, :],
                    in_=S2b[:, sl * 1024:(sl + 1) * 1024]).then_inc(osems[sl], 16)
    vv.wait(osems[0])
    vv.wait(osems[1])
    s.wait(osems[0])
    s.wait(osems[1])
    sy.wait(osems[0])
    sy.wait(osems[1])


def st_l2_ffn(ks, W, xn_dram, xnT_dram, f_dram):
    v, s, t, sy, g = (ks.r[e] for e in ("vector", "scalar", "tensor", "sync", "gpsimd"))
    A1, B1, B2, C1, S1, S2, CB, WST = (ks.sb[k] for k in
                                       ("A1", "B1", "B2", "C1", "S1", "S2", "CB", "WST"))
    A1b = A1.bitcast(BF16)
    MI = ks.sb["MI"]
    PT, PQ, PE1 = ks.sb["PT"], ks.sb["PQ"], ks.sb["PE1"]
    ident, identf, ltri, onesb = (ks.sb[k] for k in ("ident", "identf", "ltri", "onesb"))
    idx_d, gat_d = ks.dram["idx_d"], ks.dram["gat_d"]
    xnbf_d = ks.dram["xn_bf"]
    thr_d = ks.dram["thr"]
    ld = ks.sem()
    mm = MM(ks)

    zst = ks.sem()
    v.memset(C1[:, 0:D], 0.0)
    ztok = ks.tok("vector")
    s.wait_tok(ztok)
    for tt in range(NT):
        s.dma_start(out=tm_dram(f_dram[0:T, :])[:, tt, :],
                    in_=C1[:, 0:D]).then_inc(zst, 16)
    s.dma_start(out=f_dram[T:T + 1, :], in_=C1[0:1, 0:D]).then_inc(zst, 16)
    s.dma_start(out=xnbf_d[T:T + 1, :], in_=C1.bitcast(BF16)[0:1, 0:D]).then_inc(zst, 16)
    v.memset(MI[:, 0:32], 2048)
    v.memset(S2[:, 0:32], 0.0)
    itok = ks.tok("vector")
    s.wait_tok(itok)
    s.dma_start(out=bass.AP(idx_d.tensor, 0, [[32, 128], [1, 32]]),
                in_=MI[:, 0:32]).then_inc(zst, 16)
    s.dma_start(out=bass.AP(gat_d.tensor, 0, [[32, 128], [1, 32]]),
                in_=S2[:, 0:32]).then_inc(zst, 16)

    v.wait(zst)
    conv_dram_bf16(ks, W["w_gate"][:, :, :].rearrange("e i d -> (e i) d"),
                   ks.dram["wg_bf"], 8 * 4096, D)
    conv_dram_bf16(ks, W["w_up"][:, :, :].rearrange("e i d -> (e i) d"),
                   ks.dram["wu_bf"], 8 * 4096, D)
    wdsrc = W["w_down"][:, :, :].rearrange("e d i -> (e d) i")
    for q in range(4):
        conv_dram_bf16(ks, bass.AP(wdsrc.tensor, q * 1024, [[4096, 8 * D], [1, 1024]]),
                       bass.AP(ks.dram["wd_bf"].tensor, q * 1024, [[4096, 8 * D], [1, 1024]]),
                       8 * D, 1024)

    load_xnT(ks, xnT_dram, B1, ld)
    ldg = ks.sem()
    sy.dma_start(out=S1[0:8, 0:1024], in_=W["gate_w"][:, :]).then_inc(ldg, 16)
    v.wait(ldg)
    v.tensor_copy(S2.bitcast(BF16)[0:8, 0:1024], S1[0:8, 0:1024])
    gb_tok = ks.tok("vector")
    t.wait_tok(gb_tok)
    t.wait(ld)
    trs = ks.gsem()
    pc = None
    for dc in range(NCH):
        if pc is not None:
            t.wait_tok(pc)
        t.transpose(PT[0:128, 0:8], S2.bitcast(BF16)[0:8, dc * 128:(dc + 1) * 128],
                    ident[0:8, 0:8]).then_inc(trs, 1)
        v.wait(trs)
        v.tensor_copy(CB[:, dc * 8:(dc + 1) * 8], PT[0:128, 0:8])
        pc = ks.tok("vector")
    gwt_tok = ks.tok("vector")
    t.wait_tok(gwt_tok)
    AFF = C1[:, 0:128]
    afftok = None
    for tt in range(NT):
        if afftok is not None:
            t.wait_tok(afftok)
        for kc in range(NCH):
            mi = t.matmul(PQ[:, 0:8], B1[:, kc * T + tt * 128:kc * T + (tt + 1) * 128],
                          CB[:, kc * 8:(kc + 1) * 8], start=(kc == 0), stop=(kc == NCH - 1))
        mi.then_inc(trs, 1)
        v.wait(trs)
        v.reduce_max(S1[:, 0:1], PQ[:, 0:8], AX.X)
        v.tensor_scalar(S2[:, 0:8], PQ[:, 0:8], S1[:, 0:1], None, ALU.subtract)
        s.wait_tok(ks.tok("vector"))
        s.activation(S2[:, 8:16], S2[:, 0:8], AF.Exp)
        v.wait_tok(ks.tok("scalar"))
        v.reduce_sum(S1[:, 1:2], S2[:, 8:16], AX.X)
        v.reciprocal(S1[:, 2:3], S1[:, 1:2])
        v.tensor_scalar(AFF[:, tt * 8:(tt + 1) * 8], S2[:, 8:16], S1[:, 2:3],
                        None, ALU.mult)
        afftok = ks.tok("vector")
    aff_tok = ks.tok("vector")
    t.wait_tok(aff_tok)
    pc = None
    for tt in range(NT):
        if pc is not None:
            t.wait_tok(pc)
        t.transpose(PE1[0:8, 0:128], AFF[:, tt * 8:(tt + 1) * 8].bitcast(F32),
                    identf[:, :]).then_inc(trs, 1)
        v.wait(trs)
        v.tensor_copy(WST[0:8, tt * 128:(tt + 1) * 128], PE1[0:8, 0:128])
        pc = ks.tok("vector")
    lo, hi, mid, cnt, cgef, tmp = (S2[0:8, c:c + 1] for c in range(6))
    msk2 = S1[0:8, 0:2048]
    v.memset(lo, 0.0)
    v.memset(hi, 1.0)
    for it in range(40):
        v.tensor_add(mid, lo, hi)
        v.tensor_scalar(mid, mid, 0.5, None, ALU.mult)
        v.tensor_scalar(msk2, WST[0:8, 0:2048], mid, None, ALU.is_ge)
        v.reduce_sum(cnt, msk2, AX.X)
        v.tensor_scalar(cgef, cnt, 512.0, None, ALU.is_ge)
        v.tensor_sub(tmp, mid, lo)
        v.tensor_mul(tmp, tmp, cgef)
        v.tensor_add(lo, lo, tmp)
        v.tensor_sub(tmp, mid, hi)
        v.scalar_tensor_tensor(tmp, cgef, 1.0, tmp, ALU.subtract, ALU.mult)
        v.tensor_sub(hi, hi, tmp)
    thr_tok = ks.tok("vector")
    s.wait_tok(thr_tok)
    s.dma_start(out=thr_d[:].rearrange("(p one) -> p one", one=1), in_=lo).then_inc(ldg, 16)
    sy.wait(ldg)
    sy.dma_start(out=S1[:, 8:16], in_=bcast_dram_row(thr_d, 8)).then_inc(ldg, 16)
    v.wait(ldg)

    g.iota(MI[:, 32:48], pattern=[[128, 16]], base=0, channel_multiplier=1)
    iota_tok = ks.tok("gpsimd")
    v.memset(S2[32:40, 0:16], 0.0)
    v.memset(S2[:, 16:32], 0.0)
    zs_tok = ks.tok("vector")
    scat_sems = []
    for e in range(8):
        if scat_sems:
            v.wait(scat_sems[-1])
        me = S1[:, 16 + 0:16 + 16]
        ge_ = S1[:, 32:48]
        rk = S1[:, 48:64]
        affsl = bass.AP(C1.tensor, e, [[8192, 128], [8, 16]])
        v.tensor_scalar(me, affsl, S1[:, 8 + e:9 + e], None, ALU.is_ge)
        v.tensor_tensor(ge_, affsl, me, ALU.mult)
        rtok = ks.tok("vector")
        v.reduce_sum(S2[:, 40:41], me, AX.X)
        v.tensor_copy(S2.bitcast(BF16)[:, 96:97], S2[:, 40:41])
        v.wait_tok(zs_tok)
        ptok = ks.tok("vector")
        t.wait_tok(ptok)
        t.matmul(PE1[:, 0:1], ltri[:, :], S2.bitcast(BF16)[:, 96:97],
                 start=True, stop=True).then_inc(trs, 1)
        v.wait(trs)
        v.tensor_tensor_scan(rk, me, S2[:, 16:32], 0.0, ALU.add, ALU.add)
        v.tensor_scalar(rk, rk, PE1[:, 0:1], None, ALU.add)
        v.tensor_scalar(S2[:, 48:64], rk, 512.0, None, ALU.is_le)
        v.tensor_tensor(S2[:, 48:64], S2[:, 48:64], me, ALU.mult)
        v.tensor_scalar(rk, rk, float(e * 512 - 1 - 4096), None, ALU.add)
        v.tensor_tensor(rk, rk, S2[:, 48:64], ALU.mult)
        v.tensor_scalar(rk, rk, 4096.0, None, ALU.add)
        v.tensor_copy(MI[:, 48:64].bitcast(I32), rk)
        sl_tok = ks.tok("vector")
        g.wait_tok(sl_tok)
        g.wait_tok(iota_tok)
        ssem = ks.gsem()
        scat_sems.append(ssem)
        for tt in range(NT):
            g.indirect_dma_start(out=idx_d[:, :],
                                 out_offset=bass.IndirectOffsetOnAxis(MI[:, 48 + tt:49 + tt], 0),
                                 in_=MI[:, 32 + tt:33 + tt], in_offset=None,
                                 oob_is_err=False).then_inc(ssem, 16)
            g.indirect_dma_start(out=gat_d[:, :],
                                 out_offset=bass.IndirectOffsetOnAxis(MI[:, 48 + tt:49 + tt], 0),
                                 in_=ge_[:, tt:tt + 1], in_offset=None,
                                 oob_is_err=False).then_inc(ssem, 16)
    g.wait(zst)
    for ssem in scat_sems:
        g.wait(ssem)
    ks.barrier()

    wg_bf, wu_bf, wd_bf = ks.dram["wg_bf"], ks.dram["wu_bf"], ks.dram["wd_bf"]
    ysc_sems = []
    for e in range(8):
        lids = ks.sem()
        sy.dma_start(out=MI[:, 0:4],
                     in_=bass.AP(idx_d.tensor, e * 512, [[4, 128], [1, 4]])).then_inc(lids, 16)
        sy.dma_start(out=S1[:, 0:4],
                     in_=bass.AP(gat_d.tensor, e * 512, [[4, 128], [1, 4]])).then_inc(lids, 16)
        g.wait(lids)
        v.memset(CB[:, 0:4096], 0.0)
        g.wait_tok(ks.tok("vector"))
        gsem = ks.gsem()
        for sc in range(4):
            g.indirect_dma_start(out=CB[:, sc * 1024:(sc + 1) * 1024],
                                 out_offset=None, in_=xnbf_d[:, :],
                                 in_offset=bass.IndirectOffsetOnAxis(MI[:, sc:sc + 1], 0),
                                 oob_is_err=False).then_inc(gsem, 16)
        t.wait(gsem)
        prevc = None
        for sc in range(4):
            for dc in range(NCH):
                if prevc is not None:
                    t.wait_tok(prevc)
                t.transpose(PT[:, :], CB[:, sc * 1024 + dc * 128:sc * 1024 + (dc + 1) * 128],
                            ident[:, :]).then_inc(trs, 1)
                v.wait(trs)
                v.tensor_copy(CB[:, 4096 + dc * 512 + sc * 128:4096 + dc * 512 + (sc + 1) * 128],
                              PT[:, :])
                prevc = ks.tok("vector")
        t.wait_tok(prevc)
        wl = ks.sem()
        for dc in range(NCH):
            sy.dma_start_transpose(A1b[:, dc * 4096:(dc + 1) * 4096],
                                   wg_bf[e * 4096:(e + 1) * 4096, dc * 128:(dc + 1) * 128]
                                   ).then_inc(wl, 16)
        t.wait(wl)
        mmve = ks.sem()
        freet = [None] * 4
        for ic in range(32):
            b = ic % 4
            if freet[b] is not None:
                t.wait_tok(freet[b])
            ps = mm.PS[b]
            for dc in range(NCH):
                mi = t.matmul(ps[:, :], A1b[:, dc * 4096 + ic * 128:dc * 4096 + (ic + 1) * 128],
                              CB[:, 4096 + dc * 512:4096 + (dc + 1) * 512],
                              start=(dc == 0), stop=(dc == NCH - 1))
            mi.then_inc(mmve, 1)
            v.wait(mmve)
            v.tensor_copy(B2[:, ic * 512:(ic + 1) * 512], ps[:, :])
            freet[b] = ks.tok("vector")
        wl2 = ks.sem()
        sy.wait(mmve, mmve.n)
        for dc in range(NCH):
            sy.dma_start_transpose(A1b[:, dc * 4096:(dc + 1) * 4096],
                                   wu_bf[e * 4096:(e + 1) * 4096, dc * 128:(dc + 1) * 128]
                                   ).then_inc(wl2, 16)
        t.wait(wl2)
        prev_tt = {}
        for ic in range(32):
            b = ic % 4
            if freet[b] is not None:
                t.wait_tok(freet[b])
            ps = mm.PS[b]
            for dc in range(NCH):
                mi = t.matmul(ps[:, :], A1b[:, dc * 4096 + ic * 128:dc * 4096 + (ic + 1) * 128],
                              CB[:, 4096 + dc * 512:4096 + (dc + 1) * 512],
                              start=(dc == 0), stop=(dc == NCH - 1))
            mi.then_inc(mmve, 1)
            sl2 = (ic % 2) * 512
            s.wait(mmve)
            if prev_tt.get(ic % 2):
                s.wait_tok(prev_tt[ic % 2])
            s.activation(S2[:, sl2:sl2 + 512], B2[:, ic * 512:(ic + 1) * 512], AF.Silu)
            v.wait_tok(ks.tok("scalar"))
            v.tensor_tensor(B2[:, ic * 512:(ic + 1) * 512], S2[:, sl2:sl2 + 512], ps[:, :],
                            ALU.mult)
            freet[b] = ks.tok("vector")
            prev_tt[ic % 2] = freet[b]
        wl3 = ks.sem()
        sy.wait(mmve, mmve.n)
        for ic in range(32):
            sy.dma_start_transpose(A1b[:, ic * 1024:(ic + 1) * 1024],
                                   wd_bf[e * D:(e + 1) * D, ic * 128:(ic + 1) * 128]
                                   ).then_inc(wl3, 16)
        t.wait(wl3)
        ysem = ks.gsem()
        ysc_sems.append(ysem)
        t.wait_tok(ks.tok("vector"))
        yfree = list(freet)
        for tt in range(4):
            for oc in range(2):
                b = (tt * 2 + oc) % 4
                if yfree[b] is not None:
                    t.wait_tok(yfree[b])
                ps = mm.PS[b]
                for ic in range(32):
                    mi = t.matmul(ps[:, :], B2[:, ic * 512 + tt * 128:ic * 512 + (tt + 1) * 128],
                                  A1b[:, ic * 1024 + oc * 512:ic * 1024 + (oc + 1) * 512],
                                  start=(ic == 0), stop=(ic == 31))
                mi.then_inc(mmve, 1)
                v.wait(mmve)
                if oc == 0 and tt > 0:
                    v.wait(ysem)
                v.tensor_scalar(C1[:, 4096 + oc * 512:4096 + (oc + 1) * 512], ps[:, :],
                                S1[:, tt:tt + 1], None, ALU.mult)
                yfree[b] = ks.tok("vector")
                if oc == 1:
                    g.wait_tok(ks.tok("vector"))
                    g.indirect_dma_start(out=f_dram[:, :],
                                         out_offset=bass.IndirectOffsetOnAxis(MI[:, tt:tt + 1], 0),
                                         in_=C1[:, 4096:4096 + D], in_offset=None,
                                         oob_is_err=False,
                                         compute_op=ALU.add).then_inc(ysem, 16)
        sy.wait(ysem)
        g.wait(ysem)



def st_l1_ffn(ks, W, xnT_dram, f_dram):
    v, s, t, sy, g = (ks.r[e] for e in ("vector", "scalar", "tensor", "sync", "gpsimd"))
    A1, B1, B2, C1, S1, S2, CB, WST = (ks.sb[k] for k in
                                       ("A1", "B1", "B2", "C1", "S1", "S2", "CB", "WST"))
    A1b = A1.bitcast(BF16)
    B2f = B2.bitcast(F32)
    PT = ks.sb["PT"]
    PS = [ks.sb[p] for p in ("PS0", "PS1", "PS2", "PS3")]
    ident = ks.sb["ident"]
    h_d = ks.dram["h_scr"]
    ld = ks.sem()
    trs = ks.gsem()

    load_xnT(ks, xnT_dram, B1, ld)
    m1tok = load_w_bf16(ks, W["m1"], CB[:, 0:512], ncols=64, nkc=8)
    ldm = ks.sem()
    mtok = [m1tok, m1tok]
    for q in range(4):
        sl = q % 2
        sy.wait_tok(mtok[sl])
        sy.dma_start(out=WST[0:64, sl * 1024:(sl + 1) * 1024],
                     in_=W["m2"][:, :, :].rearrange("l x y -> l (x y)")
                     [:, q * 1024:(q + 1) * 1024]).then_inc(ldm, 16)
        v.wait(ldm)
        v.tensor_copy(CB[0:64, 512 + q * 1024:512 + (q + 1) * 1024],
                      WST[0:64, sl * 1024:(sl + 1) * 1024])
        mtok[sl] = ks.tok("vector")
    w1v = W["w1"][:, :, :].rearrange("d x y -> d (x y)")
    ldw = ks.sem()
    wt2 = [mtok[0], mtok[1]]
    for dc in range(NCH):
        for q in range(4):
            sl = q % 2
            sy.wait_tok(wt2[sl])
            sy.dma_start(out=WST[:, sl * 1024:(sl + 1) * 1024],
                         in_=w1v[dc * 128:(dc + 1) * 128, q * 1024:(q + 1) * 1024]
                         ).then_inc(ldw, 16)
            v.wait(ldw)
            v.tensor_copy(A1b[:, dc * 4096 + q * 1024:dc * 4096 + (q + 1) * 1024],
                          WST[:, sl * 1024:(sl + 1) * 1024])
            wt2[sl] = ks.tok("vector")
    sy.dma_start(out=C1[:, 0:4096],
                 in_=bass.AP(W["mb"].tensor, 0, [[0, 128], [1, 4096]])).then_inc(ldm, 16)
    sy.dma_start(out=C1[:, 4096:4096 + D], in_=bcast_dram_row(W["b2"], D)).then_inc(ldm, 16)
    v.wait(ldm)
    t.wait(ld)
    t.wait_tok(ks.tok("vector"))
    S2b = S2.bitcast(BF16)
    LRT = S2b[0:64, 0:2048]
    lr_free = None
    for tc in range(4):
        if lr_free is not None:
            t.wait_tok(lr_free)
        mi = None
        for kc in range(NCH):
            mi = t.matmul(ks.sb["PQ"][0:64, :], CB[:, kc * 64:(kc + 1) * 64],
                          B1[:, kc * T + tc * 512:kc * T + (tc + 1) * 512],
                          start=(kc == 0), stop=(kc == NCH - 1))
        mi.then_inc(trs, 1)
        v.wait(trs)
        v.tensor_copy(LRT[:, tc * 512:(tc + 1) * 512], ks.sb["PQ"][0:64, :])
        lr_free = ks.tok("vector")
    t.wait_tok(lr_free)
    bank_free = [None] * 4
    hst = ks.sem()
    MLF = B2f[:, 0:4096]
    MT = B2[:, 8192:12288]
    MAH = B2f[:, 6144:8192]
    for tt in range(NT):
        for oc in range(8):
            b = oc % 4
            if bank_free[b] is not None:
                t.wait_tok(bank_free[b])
            t.matmul(PS[b][:, :], LRT[:, tt * 128:(tt + 1) * 128],
                     CB[0:64, 512 + oc * 512:512 + (oc + 1) * 512],
                     start=True, stop=True).then_inc(trs, 1)
            v.wait(trs)
            v.scalar_tensor_tensor(MLF[:, oc * 512:(oc + 1) * 512], PS[b][:, :], 1.0,
                                   C1[:, oc * 512:(oc + 1) * 512], ALU.mult, ALU.add)
            bank_free[b] = ks.tok("vector")
        ml3 = MLF.rearrange("p (j y) -> p j y", y=64)
        v.tensor_reduce(S1[:, 0:64], ml3, AX.X, ALU.max)
        mxb = S1[:, 0:64].unsqueeze(2).broadcast_to([128, 64, 64])
        v.tensor_tensor(MT.rearrange("p (j y) -> p j y", y=64), ml3, mxb, ALU.is_ge)
        msk_tok = ks.tok("vector")
        for half in range(2):
            for oc in range(4):
                b = oc % 4
                if bank_free[b] is not None:
                    t.wait_tok(bank_free[b])
                for kc in range(NCH):
                    mi = t.matmul(PS[b][:, :], B1[:, kc * T + tt * 128:kc * T + (tt + 1) * 128],
                                  A1b[:, kc * 4096 + half * 2048 + oc * 512:
                                      kc * 4096 + half * 2048 + (oc + 1) * 512],
                                  start=(kc == 0), stop=(kc == NCH - 1))
                mi.then_inc(trs, 1)
                v.wait(trs)
                v.tensor_tensor(MAH[:, oc * 512:(oc + 1) * 512], PS[b][:, :],
                                MT[:, half * 2048 + oc * 512:half * 2048 + (oc + 1) * 512],
                                ALU.mult)
                bank_free[b] = ks.tok("vector")
            v.tensor_reduce(S1[:, 64 + half * 32:64 + (half + 1) * 32],
                            MAH.rearrange("p (j y) -> p j y", y=64), AX.X, ALU.add)
        v.tensor_scalar(S1[:, 64:128], S1[:, 64:128], 0.0, None, ALU.max)
        rb = S1[:, 64:128].unsqueeze(2).broadcast_to([128, 64, 64])
        v.tensor_tensor(MT.rearrange("p (j y) -> p j y", y=64),
                        MT.rearrange("p (j y) -> p j y", y=64), rb, ALU.mult)
        s.wait_tok(ks.tok("vector"))
        s.dma_start(out=h_d[tt * 128:(tt + 1) * 128, :], in_=MT).then_inc(hst, 16)
        sy.wait(hst)
        v.wait(hst)
    w2v = W["w2"][:, :, :].rearrange("y j c -> j y c")
    ldw2 = ks.sem()
    wt3 = [ks.tok("vector"), ks.tok("vector")]
    t.wait_tok(ks.tok("vector"))
    for ch in range(32):
        sl = ch % 2
        sy.wait_tok(wt3[sl])
        srcap = bass.AP(w2v.tensor, (ch * 2) * 1024, [[1024, 2], [64 * 1024, 64], [1, 1024]])
        sy.dma_start(out=WST[:, sl * 1024:(sl + 1) * 1024], in_=srcap).then_inc(ldw2, 16)
        v.wait(ldw2)
        v.tensor_copy(A1b[:, ch * 1024:(ch + 1) * 1024],
                      WST[:, sl * 1024:(sl + 1) * 1024])
        wt3[sl] = ks.tok("vector")
    t.wait_tok(ks.tok("vector"))
    ldh = ks.sem()
    stf_slots = [None, None]
    for tt in range(NT):
        sy.wait_tok(ks.tok("vector"))
        sy.dma_start(out=B2[:, 4096:8192], in_=h_d[tt * 128:(tt + 1) * 128, :]).then_inc(ldh, 16)
        t.wait(ldh)
        prevc = None
        for ic in range(32):
            if prevc is not None:
                t.wait_tok(prevc)
            t.transpose(PT[:, :], B2[:, 4096 + ic * 128:4096 + (ic + 1) * 128],
                        ident[:, :]).then_inc(trs, 1)
            v.wait(trs)
            v.tensor_copy(B2[:, ic * 128:(ic + 1) * 128], PT[:, :])
            prevc = ks.tok("vector")
        t.wait_tok(prevc)
        for oc in range(2):
            b = oc
            if bank_free[b] is not None:
                t.wait_tok(bank_free[b])
            for ic in range(32):
                mi = t.matmul(PS[b][:, :], B2[:, ic * 128:(ic + 1) * 128],
                              A1b[:, ic * 1024 + oc * 512:ic * 1024 + (oc + 1) * 512],
                              start=(ic == 0), stop=(ic == 31))
            mi.then_inc(trs, 1)
            v.wait(trs)
            if stf_slots[oc] is not None:
                v.wait(stf_slots[oc])
            v.scalar_tensor_tensor(C1[:, 5120 + oc * 512:5120 + (oc + 1) * 512],
                                   PS[b][:, :], 1.0,
                                   C1[:, 4096 + oc * 512:4096 + (oc + 1) * 512],
                                   ALU.mult, ALU.add)
            bank_free[b] = ks.tok("vector")
        stf = ks.sem()
        s.wait_tok(ks.tok("vector"))
        s.dma_start(out=tm_dram(f_dram)[:, tt, :],
                    in_=C1[:, 5120:5120 + D]).then_inc(stf, 16)
        stf_slots[0] = stf
        stf_slots[1] = stf
    s.wait(stf_slots[0])


def st_l1_attn(ks, W, xnT_dram, s_dram):
    v, s, t, sy, g = (ks.r[e] for e in ("vector", "scalar", "tensor", "sync", "gpsimd"))
    A1, B1, B2, C1, S1, S2, CB, WST = (ks.sb[k] for k in
                                       ("A1", "B1", "B2", "C1", "S1", "S2", "CB", "WST"))
    A1b = A1.bitcast(BF16)
    C1b = C1.bitcast(BF16)
    S2b = S2.bitcast(BF16)
    PS = [ks.sb[p] for p in ("PS0", "PS1", "PS2", "PS3")]
    PR, PQ, PE1 = ks.sb["PR"], ks.sb["PQ"], ks.sb["PE1"]
    PT = ks.sb["PT"]
    onesb, onesf, ident = ks.sb["onesb"], ks.sb["onesf"], ks.sb["ident"]
    qT_d = ks.dram["scr_q"]
    sT_d = ks.dram["scr_k"]
    ld = ks.sem()

    conv_dram_bf16(ks, W["qkv_w"], ks.dram["qkvw_bf"], 3072, D)
    conv_dram_bf16(ks, W["proj_w"], ks.dram["projw_bf"], D, D)
    load_xnT(ks, xnT_dram, B1, ld)
    wl = ks.sem()
    for dc in range(NCH):
        sy.dma_start_transpose(A1b[:, dc * 3072:(dc + 1) * 3072],
                               ks.dram["qkvw_bf"][:, dc * 128:(dc + 1) * 128]).then_inc(wl, 16)
    ldb = ks.sem()
    sy.wait_tok(ks.tok("vector"))
    sy.dma_start(out=S1[:, 0:24],
                 in_=bass.AP(W["qkv_b"].tensor, 0, [[1, 128], [128, 24]]),
                 allow_slow_non_contiguous=True).then_inc(ldb, 16)
    sy.dma_start(out=WST[:, 0:1024],
                 in_=bass.AP(W["qkv_b"].tensor, 1024, [[0, 128], [1, 1024]])).then_inc(ldb, 16)
    sy.dma_start(out=WST[:, 1024:2048],
                 in_=bass.AP(W["qkv_b"].tensor, 2048, [[0, 128], [1, 1024]])).then_inc(ldb, 16)
    t.wait(ld)
    t.wait(wl)
    s.wait(ldb)
    v.wait(ldb)

    trs = ks.gsem()
    qst = ks.sem()
    bank_free = [None] * 4
    for ofc in range(NCH):
        for kc in range(NCH):
            for tc in range(4):
                b = tc
                if kc == 0 and bank_free[b] is not None:
                    t.wait_tok(bank_free[b])
                mi = t.matmul(PS[tc][:, :],
                              A1b[:, kc * 3072 + ofc * 128:kc * 3072 + (ofc + 1) * 128],
                              B1[:, kc * T + tc * 512:kc * T + (tc + 1) * 512],
                              start=(kc == 0), stop=(kc == NCH - 1))
                if kc == NCH - 1:
                    mi.then_inc(trs, 1)
        for tc in range(4):
            s.wait(trs)
            s.activation(B2[:, ofc * T + tc * 512:ofc * T + (tc + 1) * 512],
                         PS[tc][:, :], AF.Identity, bias=S1[:, ofc:ofc + 1])
            bank_free[tc] = ks.tok("scalar")
    s.wait_tok(ks.tok("scalar"))
    s.dma_start(out=bass.AP(qT_d.tensor, 0, [[T, 128], [128 * T, NCH], [1, T]]),
                in_=B2[:, :].rearrange("p (c t) -> p c t", t=T)).then_inc(qst, 16)

    def tokmaj_proj(off, bias_ap_col, dest_fn, waitsem=None):
        for tt in range(NT):
            for oc in range(2):
                b = (tt * 2 + oc) % 4
                if bank_free[b] is not None:
                    t.wait_tok(bank_free[b])
                for kc in range(NCH):
                    mi = t.matmul(PS[b][:, :],
                                  B1[:, kc * T + tt * 128:kc * T + (tt + 1) * 128],
                                  A1b[:, kc * 3072 + off + oc * 512:kc * 3072 + off + (oc + 1) * 512],
                                  start=(kc == 0), stop=(kc == NCH - 1))
                mi.then_inc(trs, 1)
                v.wait(trs)
                if waitsem is not None and tt == 0 and oc == 0:
                    v.wait(waitsem)
                v.scalar_tensor_tensor(dest_fn(tt)[:, oc * 512:(oc + 1) * 512],
                                       PS[b][:, :], 1.0,
                                       bias_ap_col[:, oc * 512:(oc + 1) * 512],
                                       ALU.mult, ALU.add)
                bank_free[b] = ks.tok("vector")

    tokmaj_proj(1024, WST[:, 0:1024], lambda tt: tmv(C1b, tt))
    tokmaj_proj(2048, WST[:, 1024:2048], lambda tt: tmv(B2, tt), waitsem=qst)
    ks.barrier()

    QRS = [A1b[:, i * 1024:(i + 1) * 1024] for i in range(8)]
    DD = [A1b[:, 8192 + i * 1024:8192 + (i + 1) * 1024] for i in range(8)]
    PTILES = A1b[:, 16384:32768]
    PTFb = ks.sb["PTF"]
    jt_ps = [(PS[0], PS[1]), (PS[2], PS[3]), (PR, PQ), (PE1, PTFb)]
    qr_sems = [ks.gsem() for _ in range(8)]
    qr_toks = [None] * 8
    dd_toks = [None] * 8
    qr_cnt = [0] * 8
    sT_st = ks.sem()
    dma_engs = [sy, s]
    dd_idx = [0]
    for h in range(H):
        v.wait_tok(ks.tok("scalar"))
        for jt in range(NT):
            v.tensor_scalar(WST[:, jt * 64:(jt + 1) * 64],
                            tmv(C1b, jt)[:, h * 64:(h + 1) * 64], -1.0, None, ALU.mult)
        nk_tok = ks.tok("vector")
        s.wait_tok(nk_tok)
        for ih in range(2):
            for jg in range(4):
                jts = list(range(jg * 4, jg * 4 + 4))
                dacts = [d for d in range(64) if (d % 5) < 3]
                ddves = [d for d in range(64) if (d % 5) >= 3]
                act_pos = {d: i for i, d in enumerate(dacts)}
                dve_pos = {d: i for i, d in enumerate(ddves)}

                def issue_qr(d2, sl2, de2):
                    if qr_toks[sl2] is not None:
                        de2.wait_tok(qr_toks[sl2])
                    de2.dma_start(out=QRS[sl2],
                                  in_=bass.AP(qT_d.tensor, (h * 64 + d2) * T + ih * 1024,
                                              [[0, 128], [1, 1024]])).then_inc(qr_sems[sl2], 16)
                    qr_cnt[sl2] = qr_sems[sl2].n
                    qr_wait[sl2] = qr_cnt[sl2]

                qr_wait = [0] * 8
                for k in range(4):
                    issue_qr(dacts[k], k, s)
                    issue_qr(ddves[k], 4 + k, sy)
                for d in range(64):
                    use_act = (d % 5) < 3
                    if use_act:
                        i2 = act_pos[d]
                        sl = i2 % 4
                    else:
                        i2 = dve_pos[d]
                        sl = 4 + i2 % 4
                    eng = s if use_act else v
                    eng.op("wait_ge", qr_sems[sl], qr_wait[sl])
                    for ji, jt in enumerate(jts):
                        dsl = (0 if use_act else 4) + dd_idx[0] % 4
                        dd_idx[0] += 1
                        if dd_toks[dsl] is not None:
                            t_ = dd_toks[dsl]
                            eng.op("wait_ge", t_[0], t_[1])
                        if use_act:
                            s.activation(DD[dsl], QRS[sl], AF.Abs,
                                         bias=WST[:, jt * 64 + d:jt * 64 + d + 1])
                            ptok = ks.tok("scalar")
                        else:
                            v.tensor_scalar(DD[dsl], QRS[sl],
                                            WST[:, jt * 64 + d:jt * 64 + d + 1],
                                            None, ALU.add)
                            v.scalar_tensor_tensor(DD[dsl], DD[dsl], -1.0, DD[dsl],
                                                   ALU.mult, ALU.max)
                            ptok = ks.tok("vector")
                        if ji == len(jts) - 1:
                            qr_toks[sl] = ptok
                            if use_act and i2 + 4 < len(dacts):
                                issue_qr(dacts[i2 + 4], sl, s)
                            elif (not use_act) and i2 + 4 < len(ddves):
                                issue_qr(ddves[i2 + 4], sl, sy)
                        t.wait_tok(ptok)
                        pa, pb = jt_ps[ji]
                        mia = t.matmul(pa[:, :], ident[:, :], DD[dsl][:, 0:512],
                                       start=(d == 0), stop=(d == 63))
                        mib = t.matmul(pb[:, :], ident[:, :], DD[dsl][:, 512:1024],
                                       start=(d == 0), stop=(d == 63))
                        mib.then_inc(trs, 1)
                        dd_toks[dsl] = (trs, trs.n)
                for ji, jt in enumerate(jts):
                    pa, pb = jt_ps[ji]
                    s.wait(trs)
                    s.activation(PTILES[:, jt * 1024:jt * 1024 + 512], pa[:, :],
                                 AF.Exp, scale=-0.0625)
                    s.activation(PTILES[:, jt * 1024 + 512:(jt + 1) * 1024], pb[:, :],
                                 AF.Exp, scale=-0.0625)
                ex_tok = ks.tok("scalar")
                t.wait_tok(ex_tok)
            t.wait_tok(ks.tok("scalar"))
            for jt in range(NT):
                for q2 in range(2):
                    ma = t.matmul(PS[q2][0:64, :], tmv(B2, jt)[:, h * 64:(h + 1) * 64],
                                  PTILES[:, jt * 1024 + q2 * 512:jt * 1024 + (q2 + 1) * 512],
                                  start=(jt == 0), stop=(jt == NT - 1))
                    mb2 = t.matmul(PE1[q2 * 64:q2 * 64 + 1, :], onesb[:, 0:1],
                                   PTILES[:, jt * 1024 + q2 * 512:jt * 1024 + (q2 + 1) * 512],
                                   start=(jt == 0), stop=(jt == NT - 1))
            ma.then_inc(trs, 1)
            mb2.then_inc(trs, 1)
            v.wait(trs)
            for q2 in range(2):
                v.tensor_scalar(WST[q2 * 64:q2 * 64 + 1, 1024:1536],
                                PE1[q2 * 64:q2 * 64 + 1, :], 1e-6, None, ALU.add)
                v.reciprocal(WST[q2 * 64:q2 * 64 + 1, 1536:2048],
                             WST[q2 * 64:q2 * 64 + 1, 1024:1536])
            t.wait_tok(ks.tok("vector"))
            for q2 in range(2):
                if q2 == 1:
                    t.wait_tok(ks.tok("scalar"))
                t.matmul(PE1[0:64, :], onesf[q2 * 64:q2 * 64 + 1, 0:64],
                         WST[q2 * 64:q2 * 64 + 1, 1536:2048],
                         start=True, stop=True).then_inc(trs, 1)
                s.wait(trs)
                s.copy(S2[0:64, 1024 + q2 * 512:1024 + (q2 + 1) * 512], PE1[0:64, :])
            v.wait_tok(ks.tok("scalar"))
            v.wait(sT_st)
            for q2 in range(2):
                v.tensor_tensor(S2b[0:64, q2 * 512:(q2 + 1) * 512],
                                PS[q2][0:64, :],
                                S2[0:64, 1024 + q2 * 512:1024 + (q2 + 1) * 512], ALU.mult)
            so_tok = ks.tok("vector")
            s.wait_tok(so_tok)
            s.dma_start(out=bass.AP(sT_d.tensor, h * 64 * T + ih * 1024,
                                    [[T, 64], [1, 1024]]),
                        in_=S2b[0:64, 0:1024]).then_inc(sT_st, 16)
            t.wait_tok(so_tok)
    ks.barrier()

    ld2 = ks.sem()
    sy.dma_start(out=A1b[:, 0:16384].rearrange("p (c t) -> p c t", t=T),
                 in_=bass.AP(sT_d.tensor, 0, [[T, 128], [128 * T, NCH], [1, T]])).then_inc(ld2, 16)
    ldc = ks.sem()
    sy.dma_start(out=S1[:, 0:24],
                 in_=bass.AP(W["dwc_w"].tensor, 0, [[3, 128], [384, 8], [1, 3]])
                 ).then_inc(ldc, 16)
    sy.dma_start(out=S1[:, 24:32],
                 in_=bass.AP(W["dwc_b"].tensor, 0, [[1, 128], [128, 8]]),
                 allow_slow_non_contiguous=True).then_inc(ldc, 16)
    v.wait(ld2)
    v.wait(ldc)
    prevc = None
    for kc in range(NCH):
        for tt in range(NT):
            if prevc is not None:
                t.wait_tok(prevc)
            t.transpose(PT[:, :], tmv(B2, tt)[:, kc * 128:(kc + 1) * 128],
                        ident[:, :]).then_inc(trs, 1)
            v.wait(trs)
            v.tensor_copy(S2b[:, tt * 128:(tt + 1) * 128], PT[:, :])
            prevc = ks.tok("vector")
        vt = S2b[:, 0:2048]
        v.tensor_scalar(WST[:, 0:2048], vt, S1[:, kc * 3 + 1:kc * 3 + 2], None, ALU.mult)
        v.scalar_tensor_tensor(WST[:, 1:2048], vt[:, 0:2047], S1[:, kc * 3:kc * 3 + 1],
                               WST[:, 1:2048], ALU.mult, ALU.add)
        v.scalar_tensor_tensor(WST[:, 0:2047], vt[:, 1:2048], S1[:, kc * 3 + 2:kc * 3 + 3],
                               WST[:, 0:2047], ALU.mult, ALU.add)
        v.tensor_scalar(WST[:, 0:2048], WST[:, 0:2048], S1[:, 24 + kc:25 + kc], None, ALU.add)
        v.tensor_tensor(A1b[:, kc * T:(kc + 1) * T], A1b[:, kc * T:(kc + 1) * T],
                        WST[:, 0:2048], ALU.add)
        prevc = ks.tok("vector")
    yt_tok = ks.tok("vector")
    wl3 = ks.sem()
    for dc in range(NCH):
        sy.dma_start_transpose(CB[:, dc * 1024:(dc + 1) * 1024],
                               ks.dram["projw_bf"][:, dc * 128:(dc + 1) * 128]).then_inc(wl3, 16)
    ldp = ks.sem()
    sy.dma_start(out=C1[:, 4 * D:5 * D], in_=bcast_dram_row(W["proj_b"], D)).then_inc(ldp, 16)
    t.wait(wl3)
    t.wait_tok(yt_tok)
    v.wait(ldp)
    mm = MM(ks)

    def actT3(kc, tt):
        return A1b[:, kc * T + tt * 128:kc * T + (tt + 1) * 128]

    def wfn3(kc, oc):
        return CB[:, kc * D + oc * 512:kc * D + (oc + 1) * 512]

    slot_sems = [None, None, None, None]

    def evac3(tt, oc, ps, mmv):
        if oc == 0 and slot_sems[tt % 4] is not None:
            v.wait(slot_sems[tt % 4])
        v.wait(mmv)
        v.scalar_tensor_tensor(
            tmv(C1, tt % 4)[:, oc * 512:(oc + 1) * 512], ps[:, :], 1.0,
            C1[:, 4 * D + oc * 512:4 * D + (oc + 1) * 512],
            ALU.mult, ALU.add)
        if oc == 1:
            stf = ks.sem()
            s.wait_tok(ks.tok("vector"))
            s.dma_start(out=tm_dram(s_dram)[:, tt, :],
                        in_=tmv(C1, tt % 4)).then_inc(stf, 16)
            slot_sems[tt % 4] = stf

    mm.run(actT3, wfn3, evac3, 2)
    for ssem in slot_sems:
        s.wait(ssem)



def st_l0_attn(ks, W, xn_dram, xnT_dram, s_dram):
    v, s, t, sy, g = (ks.r[e] for e in ("vector", "scalar", "tensor", "sync", "gpsimd"))
    A1, B1, B2, C1, S1, S2, CB, WST, EXT = (ks.sb[k] for k in
        ("A1", "B1", "B2", "C1", "S1", "S2", "CB", "WST", "EXT"))
    A1b, C1b, S2b, WSTb = A1.bitcast(BF16), C1.bitcast(BF16), S2.bitcast(BF16), WST.bitcast(BF16)
    EXTf = EXT.bitcast(F32)
    PS = [ks.sb[p] for p in ("PS0", "PS1", "PS2", "PS3")]
    PO = [ks.sb[p] for p in ("PR", "PQ", "PE1", "PS3")]
    PT = ks.sb["PT"]
    ident, identf = ks.sb["ident"], ks.sb["identf"]
    lw_d = ks.dram["lw_d"]
    g_d = ks.dram["g_d"]
    v_d = ks.dram["scr_v"]
    kh_d = ks.dram["kh_d"]
    rtT_d, ktT_d, roT_d = ks.dram["rtT_d"], ks.dram["ktT_d"], ks.dram["roT_d"]
    pcT_d = ks.dram["pcT_d"]
    trs = ks.gsem()

    ldr = ks.sem()
    sy.dma_start(out=a3d(A1), in_=tm_dram(xn_dram)).then_inc(ldr, 16)
    sy.dma_start(out=C1[:, 0:D], in_=bcast_dram_row(W["ln1_w"], D)).then_inc(ldr, 16)
    v.wait(ldr)
    a3 = A1.rearrange("p (n d) -> p n d", d=D)
    Bf = B1.bitcast(F32)
    for half in range(2):
        v.tensor_mul(Bf[:, :], A1[:, half * 8192:(half + 1) * 8192],
                     A1[:, half * 8192:(half + 1) * 8192])
        v.tensor_reduce(S1[:, half * 8:(half + 1) * 8],
                        Bf.rearrange("p (n d) -> p n d", d=D), AX.X, ALU.add)
    v.tensor_scalar(S1[:, 0:16], S1[:, 0:16], 1.0 / D, 1e-6, ALU.mult, ALU.add)
    s.wait_tok(ks.tok("vector"))
    s.sqrt(S1[:, 16:32], S1[:, 0:16])
    v.wait_tok(ks.tok("scalar"))
    v.reciprocal(S1[:, 0:16], S1[:, 16:32])
    rstd_b = S1[:, 0:16].unsqueeze(2).broadcast_to([128, NT, D])
    v.tensor_tensor(a3, a3, rstd_b, ALU.mult)
    lnw_b = C1[:, 0:D].unsqueeze(1).broadcast_to([128, NT, D])
    v.tensor_tensor(a3, a3, lnw_b, ALU.mult)
    v.tensor_copy(B1[:, :], A1[:, :])
    h_tok = ks.tok("vector")
    t.wait_tok(h_tok)
    hT_tok = transpose_tm_to_fm(ks, B1, B2)
    XX = A1b[:, 0:16384]
    v.wait_tok(hT_tok)
    for kc in range(NCH):
        v.tensor_sub(XX[:, kc * T + 1:(kc + 1) * T], B2[:, kc * T:(kc + 1) * T - 1],
                     B2[:, kc * T + 1:(kc + 1) * T])
        v.tensor_scalar(XX[:, kc * T:kc * T + 1], B2[:, kc * T:kc * T + 1],
                        -1.0, None, ALU.mult)
    xx_tok = ks.tok("vector")
    ldm = ks.sem()
    for i, mn in enumerate(("mu_g", "mu_w", "mu_r", "mu_k", "mu_v")):
        sy.dma_start(out=S1[:, 32 + i * 8:32 + (i + 1) * 8],
                     in_=bass.AP(W[mn].tensor, 0, [[1, 128], [128, 8]]),
                     allow_slow_non_contiguous=True).then_inc(ldm, 16)
    v.wait(ldm)

    XM = A1b[:, 16384:32768]

    def make_mix(mi):
        v.wait(trs, trs.n)
        for kc in range(NCH):
            v.scalar_tensor_tensor(XM[:, kc * T:(kc + 1) * T], XX[:, kc * T:(kc + 1) * T],
                                   S1[:, 32 + mi * 8 + kc:32 + mi * 8 + kc + 1],
                                   B2[:, kc * T:(kc + 1) * T], ALU.mult, ALU.add)
        return ks.tok("vector")

    def actX(kc, tt):
        return XM[:, kc * T + tt * 128:kc * T + (tt + 1) * 128]

    mtok = make_mix(0)
    wtok = load_w_bf16(ks, W["Wg"], CB)
    t.wait_tok(mtok)
    t.wait_tok(wtok)
    gst = ks.sem()
    bank_free = [None] * 4
    for tt in range(NT):
        for oc in range(2):
            b = (tt * 2 + oc) % 4
            if bank_free[b] is not None:
                t.wait_tok(bank_free[b])
            for kc in range(NCH):
                mi = t.matmul(PS[b][:, :], actX(kc, tt), CB[:, kc * D + oc * 512:kc * D + (oc + 1) * 512],
                              start=(kc == 0), stop=(kc == NCH - 1))
            mi.then_inc(trs, 1)
            s.wait(trs)
            if tt > 0 and oc == 0:
                s.wait(gst)
            s.activation(S2[:, oc * 512:(oc + 1) * 512], PS[b][:, :], AF.Silu)
            bank_free[b] = ks.tok("scalar")
        s.wait_tok(ks.tok("scalar"))
        s.dma_start(out=tm_dram(g_d)[:, tt, :], in_=S2[:, 0:1024]).then_inc(gst, 16)

    mtok = make_mix(1)
    waok = load_w_bf16(ks, W["wA"], CB[:, 0:512], ncols=64, nkc=8)
    ldwb = ks.sem()
    sy.wait_tok(waok)
    sy.dma_start(out=WST[0:64, 0:1024], in_=W["wB"][:, :]).then_inc(ldwb, 16)
    v.wait(ldwb)
    v.tensor_copy(CB[0:64, 512:1536], WST[0:64, 0:1024])
    sy.dma_start(out=C1[:, 1024:2048], in_=bcast_dram_row(W["w0"], D)).then_inc(ldwb, 16)
    v.wait(ldwb)
    t.wait_tok(ks.tok("vector"))
    s.wait(gst)
    lwst = ks.sem()
    lwst_c = []
    t.wait_tok(mtok)
    for tt in range(NT):
        b = tt % 2
        if bank_free[b] is not None:
            t.wait_tok(bank_free[b])
        for kc in range(NCH):
            mi = t.matmul(PS[b][0:64, 0:128], CB[:, kc * 64:(kc + 1) * 64],
                          actX(kc, tt), start=(kc == 0), stop=(kc == NCH - 1))
        mi.then_inc(trs, 1)
        s.wait(trs)
        s.activation(S2b[0:64, 0:128], PS[b][0:64, 0:128], AF.Tanh)
        t1tok = ks.tok("scalar")
        t.wait_tok(t1tok)
        for oc in range(2):
            b2_ = 2 + oc
            if bank_free[b2_] is not None:
                t.wait_tok(bank_free[b2_])
            t.matmul(PS[b2_][:, 0:512], S2b[0:64, 0:128],
                     CB[0:64, 512 + oc * 512:512 + (oc + 1) * 512],
                     start=True, stop=True).then_inc(trs, 1)
            v.wait(trs)
            if tt > 0 and oc == 0:
                v.wait(lwst, lwst_c[tt - 1])
            v.scalar_tensor_tensor(S2[:, 1024 + oc * 512:1024 + (oc + 1) * 512],
                                   PS[b2_][:, 0:512], 1.0,
                                   C1[:, 1024 + oc * 512:1024 + (oc + 1) * 512],
                                   ALU.mult, ALU.add)
            bank_free[b2_] = ks.tok("vector")
        bank_free[b] = ks.tok("vector")
        s.wait_tok(bank_free[b])
        s.activation(S2[:, 0:1024], S2[:, 1024:2048], AF.Exp)
        v.wait_tok(ks.tok("scalar"))
        v.tensor_scalar(S2[:, 1024:2048], S2[:, 0:1024], -1.0, None, ALU.mult)
        s.wait_tok(ks.tok("vector"))
        s.dma_start(out=tm_dram(lw_d)[:, tt, :], in_=S2[:, 1024:2048]).then_inc(lwst, 16)
        lwst_c.append(lwst.n)

    def proj_tok(mix_i, wname, dest_fn, store_d=None, stsem=None):
        mt = make_mix(mix_i)
        wt = load_w_bf16(ks, W[wname], CB)
        t.wait_tok(mt)
        t.wait_tok(wt)
        for tt in range(NT):
            for oc in range(2):
                b = (tt * 2 + oc) % 4
                if bank_free[b] is not None:
                    t.wait_tok(bank_free[b])
                for kc in range(NCH):
                    mi = t.matmul(PS[b][:, :], actX(kc, tt),
                                  CB[:, kc * D + oc * 512:kc * D + (oc + 1) * 512],
                                  start=(kc == 0), stop=(kc == NCH - 1))
                mi.then_inc(trs, 1)
                v.wait(trs)
                v.tensor_copy(dest_fn(tt)[:, oc * 512:(oc + 1) * 512], PS[b][:, :])
                bank_free[b] = ks.tok("vector")
            if store_d is not None:
                s.wait_tok(ks.tok("vector"))
                s.dma_start(out=tm_dram(store_d)[:, tt, :],
                            in_=dest_fn(tt)).then_inc(stsem, 16)

    proj_tok(2, "Wr", lambda tt: tmv(B1, tt))
    proj_tok(3, "Wk", lambda tt: tmv(C1b, tt))
    vst = ks.gsem()
    v.wait_tok(ks.tok("vector"))
    proj_tok(4, "Wv", lambda tt: tmv(B2, tt), store_d=v_d, stsem=vst)

    ldu = ks.sem()
    sy.wait_tok(ks.tok("vector"))
    sy.dma_start(out=WST[:, 0:1024],
                 in_=bcast_dram_row(W["u"][:, :].rearrange("a b -> (a b)"), D)).then_inc(ldu, 16)
    v.wait(ldu)
    RKU = ks.sb["MI"].bitcast(F32)
    v.wait(lwst)
    for tt in range(NT):
        v.tensor_tensor(S2[:, 0:1024], tmv(C1b, tt), WST[:, 0:1024], ALU.mult)
        v.tensor_tensor(S2[:, 1024:2048], S2[:, 0:1024], tmv(B1, tt), ALU.mult)
        v.tensor_reduce(RKU[:, tt * 16:(tt + 1) * 16],
                        S2[:, 1024:2048].rearrange("p (a b) -> p a b", b=64),
                        AX.X, ALU.add)
    ks.barrier()

    ltri8 = A1[:, 15872:16000]
    blk8 = A1[:, 16000:16128]
    m4s = A1[:, 16128:16256]
    bmaskT = A1b[:, 32512:32640]
    rm16 = A1[:, 16320:16336]
    ldc = ks.sem()
    for nm, dst in (("ltri8", ltri8), ("blk8", blk8), ("m4s", m4s), ("bmaskT", bmaskT),
                    ("rm16", rm16)):
        sy.dma_start(out=dst, in_=ks.cin[nm][:, :]).then_inc(ldc, 16)
    t.wait(ldc)
    v.wait(ldc)
    ldl = ks.sem()
    stp3 = ks.sem()
    for gi in range(NT):
        sy.wait_tok(ks.tok("vector"))
        sy.dma_start(out=S2[:, 0:1024], in_=tm_dram(lw_d)[:, gi, :]).then_inc(ldl, 16)
        t.wait(ldl)
        for oc in range(2):
            t.matmul(PS[oc][:, :], ltri8[:, :], S2[:, oc * 512:oc * 512 + 512],
                     start=True, stop=True).then_inc(trs, 1)
            t.matmul(PS[2 + oc][:, :], blk8[:, :], S2[:, oc * 512:oc * 512 + 512],
                     start=True, stop=True).then_inc(trs, 1)
        v.wait(trs)
        v.tensor_copy(S2[:, 1024:1536], PS[0][:, :])
        v.tensor_copy(S2[:, 1536:2048], PS[1][:, :])
        v.tensor_copy(EXTf[:, 0:512], PS[2][:, :])
        v.tensor_copy(EXTf[:, 512:1024], PS[3][:, :])
        pv_tok = ks.tok("vector")
        t.wait_tok(pv_tok)
        for oc in range(2):
            t.matmul(PS[oc][:, :], m4s[:, :], S2[:, 1024 + oc * 512:1024 + (oc + 1) * 512],
                     start=True, stop=True).then_inc(trs, 1)
        v.wait(trs)
        v.tensor_copy(S1[:, 0:512], PS[0][:, :])
        v.tensor_copy(S1[:, 512:1024], PS[1][:, :])
        s.wait_tok(ks.tok("vector"))
        s.activation(WST[:, 1024:2048], EXTf[:, 0:1024], AF.Exp)
        pc_tok = ks.tok("scalar")
        t.wait_tok(pc_tok)
        prevc = None
        for fc in range(NCH):
            if prevc is not None:
                t.wait_tok(prevc)
            t.transpose(PS[0][:, 0:128], WST[:, 1024 + fc * 128:1024 + (fc + 1) * 128],
                        identf[:, :]).then_inc(trs, 1)
            v.wait(trs)
            v.tensor_copy(S1[:, 1024 + fc * 16:1024 + (fc + 1) * 16],
                          bass.AP(PS[0].tensor, 0, [[512, 128], [8, 16]]))
            prevc = ks.tok("vector")
        s.wait_tok(prevc)
        s.dma_start(out=bass.AP(pcT_d.tensor, gi * 16,
                                [[256, 128], [128 * 256, 8], [1, 16]]),
                    in_=S1[:, 1024:1152].rearrange("p (c n) -> p c n", n=16)).then_inc(stp3, 16)
        def fscale(src_tiles, fexpr, dst_dram):
            v.wait_tok(ks.tok("scalar"))
            v.wait(stp3)
            fexpr()
            s.wait_tok(ks.tok("vector"))
            s.activation(CB[:, 0:1024], WST[:, 0:1024], AF.Exp)
            v.wait_tok(ks.tok("scalar"))
            v.tensor_tensor(CB[:, 1024:2048], src_tiles(gi), CB[:, 0:1024], ALU.mult)
            sc_tok = ks.tok("vector")
            t.wait_tok(sc_tok)
            prevc2 = None
            for fc in range(NCH):
                if prevc2 is not None:
                    t.wait_tok(prevc2)
                t.transpose(PT[:, :], CB[:, 1024 + fc * 128:1024 + (fc + 1) * 128],
                            ident[:, :]).then_inc(trs, 1)
                v.wait(trs)
                v.tensor_copy(CB[:, 2048 + fc * 128:2048 + (fc + 1) * 128], PT[:, :])
                prevc2 = ks.tok("vector")
            s.wait_tok(prevc2)
            s.dma_start(out=bass.AP(dst_dram.tensor, gi * 128,
                                    [[T, 128], [128 * T, NCH], [1, 128]]),
                        in_=CB[:, 2048:3072].rearrange("p (c j) -> p c j", j=128)
                        ).then_inc(stp3, 16)

        def e1():
            v.tensor_sub(WST[:, 0:1024], S2[:, 1024:2048], S1[:, 0:1024])
            v.tensor_scalar(WST[:, 0:1024], WST[:, 0:1024], 86.0, None, ALU.min)
        fscale(lambda gg: tmv(B1, gg), e1, rtT_d)
        def e2():
            v.tensor_sub(WST[:, 0:1024], S1[:, 0:1024], S2[:, 1024:2048])
            v.tensor_sub(WST[:, 0:1024], WST[:, 0:1024], S2[:, 0:1024])
            v.tensor_scalar(WST[:, 0:1024], WST[:, 0:1024], 86.0, None, ALU.min)
        fscale(lambda gg: tmv(C1b, gg), e2, ktT_d)
        def e5():
            v.tensor_copy(WST[:, 0:1024], S2[:, 1024:2048])
        fscale(lambda gg: tmv(B1, gg), e5, roT_d)
        v.wait_tok(ks.tok("scalar"))
        v.tensor_sub(WST[:, 0:1024], EXTf[:, 0:1024], S2[:, 1024:2048])
        v.tensor_sub(WST[:, 0:1024], WST[:, 0:1024], S2[:, 0:1024])
        s.wait_tok(ks.tok("vector"))
        s.activation(CB[:, 0:1024], WST[:, 0:1024], AF.Exp)
        v.wait_tok(ks.tok("scalar"))
        v.tensor_tensor(CB[:, 1024:2048], tmv(C1b, gi), CB[:, 0:1024], ALU.mult)
        s.wait_tok(ks.tok("vector"))
        s.dma_start(out=kh_d[gi * 128:(gi + 1) * 128, :],
                    in_=CB[:, 1024:2048]).then_inc(stp3, 16)
    s.wait(stp3)
    ks.barrier()

    ld4 = ks.sem()
    sy.wait(vst)
    sy.dma_start(out=a3d(B2), in_=tm_dram(kh_d)).then_inc(ld4, 16)
    sy.dma_start(out=a3d(C1b), in_=tm_dram(v_d)).then_inc(ld4, 16)
    sy.wait(ld4)
    ks.barrier()

    v.memset(EXTf[:, 0:512], 0.0)
    v.memset(WSTb[:, 0:512], 0.0)
    sw_tok = ks.tok("vector")
    t.wait_tok(sw_tok)
    for seg in range(8):
        segl = ks.sem()
        sy.wait(trs, trs.n)
        sy.wait_tok(ks.tok("vector"))
        for nm, dd in (("rt", rtT_d), ("kt", ktT_d), ("ro", roT_d)):
            off = {"rt": 0, "kt": 2048, "ro": 4096}[nm]
            sy.dma_start(out=CB[:, off:off + 2048].rearrange("p (c j) -> p c j", j=256),
                         in_=bass.AP(dd.tensor, seg * 256, [[T, 128], [128 * T, NCH], [1, 256]])
                         ).then_inc(segl, 16)
        sy.dma_start(out=S1[:, 0:256].rearrange("p (c n) -> p c n", n=32),
                     in_=bass.AP(pcT_d.tensor, seg * 32, [[256, 128], [128 * 256, 8], [1, 32]])
                     ).then_inc(segl, 16)
        t.wait(segl)
        v.wait(segl)
        for g2 in range(2):
            gg = seg * 2 + g2
            for h in range(16):
                kc = h // 2
                pb = (h % 2) * 64
                hp = h // 2
                t.wait_tok(ks.tok("vector"))
                t.matmul(PS[2][:, 0:128],
                         CB[:, 2048 + kc * 256 + g2 * 128:2048 + kc * 256 + (g2 + 1) * 128][pb:pb + 64, :],
                         CB[:, 0 + kc * 256 + g2 * 128:0 + kc * 256 + (g2 + 1) * 128][pb:pb + 64, :],
                         start=True, stop=True).then_inc(trs, 1)
                v.wait(trs)
                v.tensor_tensor(B1[:, 0:128], PS[2][:, 0:128], bmaskT[:, :], ALU.mult)
                am_tok = ks.tok("vector")
                t.wait_tok(am_tok)
                t.matmul(PS[0][pb:pb + 64, 0:128],
                         tmv(C1b, gg)[:, h * 64:(h + 1) * 64],
                         B1[:, 0:128], start=True, stop=True).then_inc(trs, 1)
                v.wait(trs)
                v.tensor_copy(S2[pb:pb + 64, hp * 256 + g2 * 128:hp * 256 + (g2 + 1) * 128],
                              PS[0][pb:pb + 64, 0:128])
            for cl in range(16):
                c = gg * 16 + cl
                ccol = g2 * 128 + cl * 8
                tile = c // 16
                v.tensor_scalar(B1[:, 1024:2048], tmv(B2, tile),
                                rm16[:, c % 16:c % 16 + 1], None, ALU.mult)
                t.wait_tok(ks.tok("vector"))
                for h in range(16):
                    kc = h // 2
                    pb = (h % 2) * 64
                    hp = h // 2
                    pob = PO[hp % 4]
                    poff = (hp // 4) * 256
                    t.matmul(pob[pb:pb + 64, poff + ccol:poff + ccol + 8],
                             WSTb[pb:pb + 64, hp * 64:(hp + 1) * 64],
                             CB[:, 4096 + kc * 256 + g2 * 128 + cl * 8:
                                4096 + kc * 256 + g2 * 128 + cl * 8 + 8][pb:pb + 64, :],
                             start=True, stop=True).then_inc(trs, 1)
                for h in range(16):
                    hp = h // 2
                    pdst = PS[h % 2]
                    pbd = (h % 2) * 64
                    t.matmul(pdst[pbd:pbd + 64, hp * 64:(hp + 1) * 64],
                             B1[:, 1024 + h * 64:1024 + (h + 1) * 64],
                             tmv(C1b, tile)[:, h * 64:(h + 1) * 64],
                             start=True, stop=True).then_inc(trs, 1)
                v.wait(trs)
                for hp in range(8):
                    v.tensor_tensor(EXTf[:, hp * 64:(hp + 1) * 64],
                                    EXTf[:, hp * 64:(hp + 1) * 64],
                                    S1[:, hp * 32 + g2 * 16 + cl:hp * 32 + g2 * 16 + cl + 1]
                                    .broadcast_to([128, 64]),
                                    ALU.mult)
                v.tensor_add(EXTf[0:64, 0:512], EXTf[0:64, 0:512], PS[0][0:64, :])
                v.tensor_add(EXTf[64:128, 0:512], EXTf[64:128, 0:512], PS[1][64:128, :])
                v.tensor_copy(WSTb[:, 0:512], EXTf[:, 0:512])
        v.wait(trs)
        for hp in range(8):
            pob = PO[hp % 4]
            poff = (hp // 4) * 256
            v.tensor_tensor(B2[:, seg * 2048 + hp * 256:seg * 2048 + (hp + 1) * 256],
                            S2[:, hp * 256:(hp + 1) * 256],
                            pob[:, poff:poff + 256], ALU.add)
    ks.barrier()
    if getattr(ks, "l0dump", False):
        dsem = ks.sem()
        sy.dma_start(out=ks.dram["dump_d"][:, :], in_=B2[:, :]).then_inc(dsem, 16)
        sy.wait(dsem)
        ks.barrier()
    emit_l0_tail(ks, W, xn_dram, s_dram)


def emit_l0_tail(ks, W, xn_dram, s_dram):
    v, s, t, sy, g = (ks.r[e] for e in ("vector", "scalar", "tensor", "sync", "gpsimd"))
    A1, B1, B2, C1, S1, S2, CB, WST, EXT = (ks.sb[k] for k in
        ("A1", "B1", "B2", "C1", "S1", "S2", "CB", "WST", "EXT"))
    A1b, C1b, S2b = A1.bitcast(BF16), C1.bitcast(BF16), S2.bitcast(BF16)
    EXTf = EXT.bitcast(F32)
    PS = [ks.sb[p] for p in ("PS0", "PS1", "PS2", "PS3")]
    PT = ks.sb["PT"]
    ident, identf = ks.sb["ident"], ks.sb["identf"]
    g_d = ks.dram["g_d"]
    x1_d = ks.dram["x1_d"]
    RKU = ks.sb["MI"].bitcast(F32)
    trs = ks.gsem()
    ld = ks.sem()
    wo_tok = load_w_bf16(ks, W["Wo"], CB)
    sy.dma_start(out=S1[:, 0:1024],
                 in_=bcast_dram_row(W["gn_w"][:, :].rearrange("a b -> (a b)"), D)).then_inc(ld, 16)
    sy.dma_start(out=S1[:, 1024:2048],
                 in_=bcast_dram_row(W["gn_b"][:, :].rearrange("a b -> (a b)"), D)).then_inc(ld, 16)
    w1_tok = load_w_bf16(ks, W["ffn_w1"], A1b, ncols=4096, nkc=NCH)
    v.wait(ld)
    t.wait_tok(wo_tok)
    gld = ks.sem()
    x1st = ks.sem()
    xnld = ks.sem()
    x1st_c = [x1st.n]
    xnld_c = []
    prev_tr = None
    prev_wo = [None] * 4
    for tt in range(NT):
        seg, half = tt // 2, tt % 2
        for hp in range(NCH):
            if prev_tr is not None:
                t.wait_tok(prev_tr)
            t.transpose(PT[:, :], B2[:, seg * 2048 + hp * 256 + half * 128:
                                     seg * 2048 + hp * 256 + half * 128 + 128],
                        ident[:, :]).then_inc(trs, 1)
            v.wait(trs)
            v.tensor_copy(B1[:, hp * 128:(hp + 1) * 128], PT[:, :])
            prev_tr = ks.tok("vector")
        sy.wait_tok(ks.tok("vector"))
        sy.wait(x1st, x1st_c[tt])
        sy.dma_start(out=S2[:, 1024:2048], in_=tm_dram(g_d)[:, tt, :]).then_inc(gld, 16)
        v.tensor_copy(S2[:, 0:1024], B1[:, 0:1024])
        for h in range(H):
            v.scalar_tensor_tensor(S2[:, h * 64:(h + 1) * 64],
                                   tmv(C1b, tt)[:, h * 64:(h + 1) * 64],
                                   RKU[:, tt * 16 + h:tt * 16 + h + 1],
                                   S2[:, h * 64:(h + 1) * 64], ALU.mult, ALU.add)
        o3 = S2[:, 0:1024].rearrange("p (a b) -> p a b", b=64)
        v.tensor_reduce(EXTf[:, 0:16], o3, AX.X, ALU.add)
        v.tensor_mul(EXTf[:, 64:1088], S2[:, 0:1024], S2[:, 0:1024])
        v.tensor_reduce(EXTf[:, 16:32],
                        EXTf[:, 64:1088].rearrange("p (a b) -> p a b", b=64),
                        AX.X, ALU.add)
        v.tensor_scalar(EXTf[:, 0:32], EXTf[:, 0:32], 1.0 / 64, None, ALU.mult)
        v.tensor_mul(EXTf[:, 32:48], EXTf[:, 0:16], EXTf[:, 0:16])
        v.tensor_sub(EXTf[:, 16:32], EXTf[:, 16:32], EXTf[:, 32:48])
        v.tensor_scalar(EXTf[:, 16:32], EXTf[:, 16:32], 1.0, 1e-5, ALU.mult, ALU.add)
        s.wait_tok(ks.tok("vector"))
        s.sqrt(EXTf[:, 32:48], EXTf[:, 16:32])
        v.wait_tok(ks.tok("scalar"))
        v.reciprocal(EXTf[:, 16:32], EXTf[:, 32:48])
        for h in range(H):
            v.tensor_scalar(S2[:, h * 64:(h + 1) * 64], S2[:, h * 64:(h + 1) * 64],
                            EXTf[:, h:h + 1], EXTf[:, 16 + h:16 + h + 1],
                            ALU.subtract, ALU.mult)
        v.tensor_mul(S2[:, 0:1024], S2[:, 0:1024], S1[:, 0:1024])
        v.tensor_add(S2[:, 0:1024], S2[:, 0:1024], S1[:, 1024:2048])
        v.wait(gld)
        if tt > 0:
            v.wait(trs, trs.n)
        v.tensor_mul(B1[:, 0:1024], S2[:, 0:1024], S2[:, 1024:2048])
        of_tok = ks.tok("vector")
        t.wait_tok(of_tok)
        for kc in range(NCH):
            if prev_tr is not None:
                t.wait_tok(prev_tr)
            t.transpose(PT[:, :], B1[:, kc * 128:(kc + 1) * 128], ident[:, :]).then_inc(trs, 1)
            v.wait(trs)
            v.tensor_copy(B1[:, 1024 + kc * 128:1024 + (kc + 1) * 128], PT[:, :])
            prev_tr = ks.tok("vector")
        sy.wait_tok(of_tok)
        sy.dma_start(out=S2[:, 1024:2048], in_=tm_dram(xn_dram)[:, tt, :]).then_inc(xnld, 16)
        xnld_c.append(xnld.n)
        t.wait_tok(prev_tr)
        for oc in range(2):
            b = (tt * 2 + oc) % 4
            if prev_wo[b] is not None:
                t.wait_tok(prev_wo[b])
            for kc in range(NCH):
                mi = t.matmul(PS[b][:, :], B1[:, 1024 + kc * 128:1024 + (kc + 1) * 128],
                              CB[:, kc * 1024 + oc * 512:kc * 1024 + (oc + 1) * 512],
                              start=(kc == 0), stop=(kc == NCH - 1))
            mi.then_inc(trs, 1)
            v.wait(trs)
            v.wait(xnld, xnld_c[tt])
            v.tensor_add(S2[:, 1024 + oc * 512:1024 + (oc + 1) * 512],
                         S2[:, 1024 + oc * 512:1024 + (oc + 1) * 512], PS[b][:, :])
            prev_wo[b] = ks.tok("vector")
        s.wait_tok(ks.tok("vector"))
        s.dma_start(out=tm_dram(x1_d)[:, tt, :], in_=S2[:, 1024:2048]).then_inc(x1st, 16)
        x1st_c.append(x1st.n)
    s.wait(x1st)
    ks.barrier()

    t.wait_tok(w1_tok)
    w2a = load_w_bf16(ks, W["ffn_w2"][0:2048, :], B1, ncols=1024, nkc=16)
    w2b = load_w_bf16(ks, W["ffn_w2"][2048:4096, :], B2, ncols=1024, nkc=16)
    ld2 = ks.sem()
    sy.dma_start(out=S1[:, 1024:2048],
                 in_=bcast_dram_row(W["ln2_w"], D)).then_inc(ld2, 16)
    v.wait(ld2)
    t.wait_tok(w2a)
    t.wait_tok(w2b)
    xld = ks.sem()
    sst = ks.sem()
    sst_c = [sst.n]
    prev_tr = None
    prev_ps = [None] * 4
    for tt in range(NT):
        sy.wait_tok(ks.tok("vector"))
        sy.dma_start(out=S2[:, 0:1024], in_=tm_dram(x1_d)[:, tt, :]).then_inc(xld, 16)
        v.wait(xld)
        v.tensor_mul(WST[:, 0:1024], S2[:, 0:1024], S2[:, 0:1024])
        v.tensor_reduce(S1[:, 0:1], WST[:, 0:1024].unsqueeze(1), AX.X, ALU.add)
        v.tensor_scalar(S1[:, 0:1], S1[:, 0:1], 1.0 / D, 1e-6, ALU.mult, ALU.add)
        s.wait_tok(ks.tok("vector"))
        s.sqrt(S1[:, 2:3], S1[:, 0:1])
        v.wait_tok(ks.tok("scalar"))
        v.reciprocal(S1[:, 0:1], S1[:, 2:3])
        v.tensor_scalar(WST[:, 0:1024], S2[:, 0:1024], S1[:, 0:1], None, ALU.mult)
        if tt > 0:
            v.wait(trs, trs.n)
        v.tensor_mul(CB[:, 5120:6144], WST[:, 0:1024], S1[:, 1024:2048])
        h2_tok = ks.tok("vector")
        t.wait_tok(h2_tok)
        for kc in range(NCH):
            if prev_tr is not None:
                t.wait_tok(prev_tr)
            t.transpose(PT[:, :], CB[:, 5120 + kc * 128:5120 + (kc + 1) * 128],
                        ident[:, :]).then_inc(trs, 1)
            v.wait(trs)
            v.tensor_copy(CB[:, kc * 128:(kc + 1) * 128], PT[:, :])
            prev_tr = ks.tok("vector")
        t.wait_tok(prev_tr)
        for gr in range(4):
            for hf in range(2):
                b = (gr * 2 + hf) % 4
                if prev_ps[b] is not None:
                    t.wait_tok(prev_ps[b])
                for kc in range(NCH):
                    mi = t.matmul(PS[b][:, :], CB[:, kc * 128:(kc + 1) * 128],
                                  A1b[:, kc * 4096 + gr * 1024 + hf * 512:
                                      kc * 4096 + gr * 1024 + (hf + 1) * 512],
                                  start=(kc == 0), stop=(kc == NCH - 1))
                mi.then_inc(trs, 1)
                s.wait(trs)
                s.activation(CB[:, 6144 + hf * 512:6144 + (hf + 1) * 512],
                             PS[b][:, :], AF.Gelu)
                prev_ps[b] = ks.tok("scalar")
            t.wait_tok(ks.tok("scalar"))
            for kc in range(NCH):
                if prev_tr is not None:
                    t.wait_tok(prev_tr)
                t.transpose(PT[:, :], CB[:, 6144 + kc * 128:6144 + (kc + 1) * 128],
                            ident[:, :]).then_inc(trs, 1)
                v.wait(trs)
                v.tensor_copy(CB[:, 1024 + gr * 1024 + kc * 128:
                                 1024 + gr * 1024 + (kc + 1) * 128], PT[:, :])
                prev_tr = ks.tok("vector")
        t.wait_tok(prev_tr)
        for oc in range(2):
            b = oc % 4
            if prev_ps[b] is not None:
                t.wait_tok(prev_ps[b])
            for q in range(32):
                wsrc = B1 if q < 16 else B2
                mi = t.matmul(PS[b][:, :], CB[:, 1024 + q * 128:1024 + (q + 1) * 128],
                              wsrc[:, (q % 16) * 1024 + oc * 512:(q % 16) * 1024 + (oc + 1) * 512],
                              start=(q == 0), stop=(q == 31))
            mi.then_inc(trs, 1)
            v.wait(trs)
            if oc == 0:
                v.wait(sst, sst_c[tt])
            v.tensor_add(S2[:, 1024 + oc * 512:1024 + (oc + 1) * 512],
                         S2[:, 0 + oc * 512:0 + (oc + 1) * 512], PS[b][:, :])
            prev_ps[b] = ks.tok("vector")
        s.wait_tok(ks.tok("vector"))
        s.dma_start(out=tm_dram(s_dram)[:, tt, :], in_=S2[:, 1024:2048]).then_inc(sst, 16)
        sst_c.append(sst.n)
    s.wait(sst)


def st_l0_ffn(ks, W, xnT_dram, f_dram):
    v, s, t, sy, g = (ks.r[e] for e in ("vector", "scalar", "tensor", "sync", "gpsimd"))
    A1, B1, B2, C1, S1, S2, CB, WST, EXT = (ks.sb[k] for k in
        ("A1", "B1", "B2", "C1", "S1", "S2", "CB", "WST", "EXT"))
    A1b, C1b, S2b = A1.bitcast(BF16), C1.bitcast(BF16), S2.bitcast(BF16)
    PS = [ks.sb[p] for p in ("PS0", "PS1", "PS2", "PS3")]
    PB = [ks.sb[p] for p in ("PS0", "PS1", "PS2", "PS3", "PR", "PQ", "PE1", "PTF")]
    PT, PTF = ks.sb["PT"], ks.sb["PTF"]
    ident, identf, onesf = ks.sb["ident"], ks.sb["identf"], ks.sb["onesf"]
    wmT_d, wpT_d, hT_d = ks.dram["wmT_d"], ks.dram["wpT_d"], ks.dram["hT_d"]
    xn1_d, xn2_d = ks.dram["xn1_d"], ks.dram["xn2_d"]
    cc_in, cc_out, flag_d = ks.dram["cc_in"], ks.dram["cc_out"], ks.dram["flag_d"]
    trs = ks.gsem()
    NCORES = ks.ncores

    wst_sem = ks.gsem()
    wst_c = [wst_sem.n]

    btp = [None]

    def block_transpose_out(src_slice_fn, nsl, stage_off, out_fn):
        for sl in range(nsl):
            if btp[0] is not None:
                t.wait_tok(btp[0])
            t.transpose(PT[:, :], src_slice_fn(sl), ident[:, :]).then_inc(trs, 1)
            v.wait(trs)
            v.tensor_copy(CB[:, stage_off + sl * 128:stage_off + (sl + 1) * 128], PT[:, :])
            btp[0] = ks.tok("vector")
        s.wait_tok(btp[0])
        out_fn()

    for fgrp in range(6):
        sy.wait(trs, trs.n)
        lt = load_w_bf16(ks, W["Wm"][fgrp * 2048:(fgrp + 1) * 2048, :], B1,
                         ncols=1024, nkc=16)
        t.wait_tok(lt)
        for kc in range(NCH):
            so = (kc % 2) * 2048
            n = fgrp * 8 + kc
            v.wait(wst_sem, wst_c[max(0, n - 1)])
            block_transpose_out(
                lambda sl, kc=kc: B1[:, sl * 1024 + kc * 128:sl * 1024 + kc * 128 + 128],
                16, so,
                lambda kc=kc, so=so: s.dma_start(
                    out=wmT_d[(fgrp * 8 + kc) * 128:(fgrp * 8 + kc + 1) * 128, :],
                    in_=CB[:, so:so + 2048]).then_inc(wst_sem, 16))
            wst_c.append(wst_sem.n)
    s.wait(wst_sem)
    ks.barrier()
    for fp in range(6):
        sy.wait(trs, trs.n)
        lt = load_w_bf16(ks, W["Wp"][:, fp * 2048:(fp + 1) * 2048], B1,
                         ncols=2048, nkc=8)
        t.wait_tok(lt)
        for fb in range(16):
            so = (fb % 2) * 1024
            qq = fp * 16 + fb
            v.wait(wst_sem, wst_c[max(0, 48 + qq - 1)])
            block_transpose_out(
                lambda cs, fb=fb: B1[:, cs * 2048 + fb * 128:cs * 2048 + fb * 128 + 128],
                8, so,
                lambda qq=qq, so=so: s.dma_start(
                    out=wpT_d[qq * 128:(qq + 1) * 128, :],
                    in_=CB[:, so:so + 1024]).then_inc(wst_sem, 16))
            wst_c.append(wst_sem.n)
    s.wait(wst_sem)
    ks.barrier()
    for gg in range(6):
        sy.wait(trs, trs.n)
        v.wait(trs, trs.n)
        lt = load_w_bf16(ks, W["Wc2"][gg * 2048:(gg + 1) * 2048, :], CB[:, 0:4096],
                         ncols=256, nkc=16)
        t.wait_tok(lt)
        for sl in range(16):
            for dc in range(2):
                if btp[0] is not None:
                    t.wait_tok(btp[0])
                t.transpose(PT[:, :], CB[:, sl * 256 + dc * 128:sl * 256 + (dc + 1) * 128],
                            ident[:, :]).then_inc(trs, 1)
                v.wait(trs)
                v.tensor_copy(A1b[:, dc * 12288 + (gg * 16 + sl) * 128:
                                  dc * 12288 + (gg * 16 + sl + 1) * 128], PT[:, :])
                btp[0] = ks.tok("vector")
    conv_dram_bf16(ks, W["Wc1"], ks.dram["wc1_bf"], 256, 1024)
    wc1l = ks.sem()
    for kc in range(NCH):
        sy.dma_start_transpose(A1b[:, 24576 + kc * 256:24576 + (kc + 1) * 256],
                               ks.dram["wc1_bf"][0:256, kc * 128:(kc + 1) * 128]
                               ).then_inc(wc1l, 16)
    sy.wait(wc1l)
    ks.barrier()

    v.memset(S1[:, 96:128], 1.0)
    v.memset(S1[:, 128:192], 0.0)
    v.memset(S1[0:1, 204:208], 0.0)
    s.wait_tok(ks.tok("vector"))
    zin = ks.sem()
    s.dma_start(out=cc_in[0:4], in_=S1[0:1, 204:208]).then_inc(zin, 16)
    s.wait(zin)
    xl = ks.sem()
    sy.dma_start(out=B2[:, :].rearrange("p (c t) -> p c t", t=T),
                 in_=bass.AP(xnT_dram.tensor, 0, [[T, 128], [128 * T, NCH], [1, T]])
                 ).then_inc(xl, 16)
    sy.wait(xl)
    ks.barrier()

    hst = ks.gsem()
    ccrd = ks.gsem()
    ldh = ks.gsem()
    ldw = ks.gsem()
    xst = ks.gsem()
    lsem = ks.gsem()
    ccs = ks.gsem()
    cst = ks.gsem()
    flst = ks.gsem()

    def pass_m(it):
        nfg = 2 if it == 1 else 6
        prev_ps = [None] * 4
        prev_stage = [None, None]
        nst = [0]
        for fgrp in range(nfg):
            sy.wait(trs, trs.n)
            for kc in range(NCH):
                sy.dma_start(out=B1[:, kc * 2048:(kc + 1) * 2048],
                             in_=wmT_d[(fgrp * 8 + kc) * 128:(fgrp * 8 + kc + 1) * 128, :]
                             ).then_inc(lsem, 16)
            t.wait(lsem, lsem.n)
            for fbl in range(16):
                fb = fgrp * 16 + fbl
                sslot = nst[0] % 2
                soff = 3072 + sslot * 2048
                for ts in range(4):
                    b = ts
                    if prev_ps[b] is not None:
                        t.wait_tok(prev_ps[b])
                    for kc in range(NCH):
                        mi = t.matmul(PS[b][:, :],
                                      B1[:, kc * 2048 + fbl * 128:kc * 2048 + fbl * 128 + 128],
                                      B2[:, kc * 2048 + ts * 512:kc * 2048 + (ts + 1) * 512],
                                      start=(kc == 0), stop=(kc == NCH - 1))
                    mi.then_inc(trs, 1)
                    s.wait(trs)
                    if ts == 0:
                        if prev_stage[sslot] is not None:
                            s.wait(hst, prev_stage[sslot])
                    s.activation(CB[:, soff + ts * 512:soff + (ts + 1) * 512],
                                 PS[b][:, :], AF.Relu)
                    prev_ps[b] = ks.tok("scalar")
                if it > 1:
                    v.wait_tok(ks.tok("scalar"))
                    v.tensor_scalar(CB[:, soff:soff + 2048], CB[:, soff:soff + 2048],
                                    S1[:, fb:fb + 1], None, ALU.mult)
                    st_tok = ks.tok("vector")
                    s.wait_tok(st_tok)
                else:
                    s.wait_tok(ks.tok("scalar"))
                s.dma_start(out=hT_d[fb * 128:(fb + 1) * 128, :],
                            in_=CB[:, soff:soff + 2048]).then_inc(hst, 16)
                nst[0] += 1
                prev_stage[sslot] = hst.n

    def pass_p(it, xdst):
        nq = 32 if it == 1 else 96
        nh = [0]
        last_mm = [0, 0]
        for ts in range(4):
            t.wait_tok(ks.tok("vector"))
            for q in range(nq):
                slot = q % 2
                sy.wait(trs, last_mm[slot])
                sy.dma_start(out=CB[:, slot * 512:(slot + 1) * 512],
                             in_=bass.AP(hT_d.tensor, q * 128 * 2048 + ts * 512,
                                         [[2048, 128], [1, 512]])).then_inc(ldh, 16)
                sy.dma_start(out=CB[:, 1024 + slot * 1024:1024 + (slot + 1) * 1024],
                             in_=wpT_d[q * 128:(q + 1) * 128, :]).then_inc(ldw, 16)
                nh[0] += 1
                t.wait(ldh, ldh.n)
                t.wait(ldw, ldw.n)
                for cb in range(8):
                    mi = t.matmul(PB[cb][:, :],
                                  CB[:, 1024 + slot * 1024 + cb * 128:
                                     1024 + slot * 1024 + cb * 128 + 128],
                                  CB[:, slot * 512:(slot + 1) * 512],
                                  start=(q == 0), stop=(q == nq - 1))
                    mi.then_inc(trs, 1)
                last_mm[slot] = trs.n
            v.wait(trs)
            for cb in range(8):
                v.tensor_copy(B2[:, cb * 2048 + ts * 512:cb * 2048 + (ts + 1) * 512],
                              PB[cb][:, :])
                v.tensor_reduce(S1[:, 256 + cb * 4 + ts:256 + cb * 4 + ts + 1],
                                PB[cb][:, :], AX.X, ALU.add)
        if xdst is not None:
            v_tok = ks.tok("vector")
            s.wait_tok(v_tok)
            for cb in range(8):
                s.dma_start(out=xdst[cb * 128:(cb + 1) * 128, :],
                            in_=B2[:, cb * 2048:(cb + 1) * 2048]).then_inc(xst, 16)
        v.tensor_reduce(S1[:, 288:296],
                        S1[:, 256:288].rearrange("p (a b) -> p a b", b=4), AX.X, ALU.add)
        v.tensor_scalar(S1[:, 288:296], S1[:, 288:296], 1.0 / T, None, ALU.mult)

    def topk(it):
        v.tensor_copy(CB[:, 3072:3080], S1[:, 288:296])
        pz_tok = ks.tok("vector")
        t.wait_tok(pz_tok)
        for kc in range(NCH):
            mi = t.matmul(PTF[0:1, 0:256], CB[:, 3072 + kc:3072 + kc + 1],
                          A1b[:, 24576 + kc * 256:24576 + (kc + 1) * 256],
                          start=(kc == 0), stop=(kc == NCH - 1))
        mi.then_inc(trs, 1)
        s.wait(trs)
        s.activation(CB[0:1, 3088:3344], PTF[0:1, 0:256], AF.Relu)
        zr_tok = ks.tok("scalar")
        t.wait_tok(zr_tok)
        prev = None
        for dc in range(2):
            if prev is not None:
                t.wait_tok(prev)
            t.transpose(PT[0:128, 0:1], CB[0:1, 3088 + dc * 128:3088 + (dc + 1) * 128],
                        ident[0:1, 0:1]).then_inc(trs, 1)
            v.wait(trs)
            v.tensor_copy(CB[:, 3344 + dc:3344 + dc + 1], PT[0:128, 0:1])
            prev = ks.tok("vector")
        t.wait_tok(prev)
        for fb in range(96):
            b = fb % 4
            if fb >= 4:
                t.wait_tok(ks.tok("vector"))
            for dc in range(2):
                mi = t.matmul(PS[b][:, 0:1],
                              A1b[:, dc * 12288 + fb * 128:dc * 12288 + (fb + 1) * 128],
                              CB[:, 3344 + dc:3344 + dc + 1],
                              start=(dc == 0), stop=(dc == 1))
            mi.then_inc(trs, 1)
            v.wait(trs)
            v.tensor_copy(S1[:, 384 + fb:384 + fb + 1], PS[b][:, 0:1])
        lo, hi, mid = (S1[:, 500 + c:501 + c] for c in range(3))
        cnt, tot, cge, ncg = (S1[:, 503 + c:504 + c] for c in range(4))
        CKM = S1[:, 384:480]
        v.tensor_reduce(S1[:, 510:511], CKM, AX.X, ALU.min)
        v.tensor_reduce(S1[:, 511:512], CKM, AX.X, ALU.max)
        mm_tok = ks.tok("vector")
        t.wait_tok(mm_tok)
        for c2 in range(2):
            if c2 > 0:
                t.wait_tok(ks.tok("vector"))
            t.transpose(PTF[0:1, 0:128], S1[:, 510 + c2:511 + c2],
                        identf[:, :]).then_inc(trs, 1)
            v.wait(trs)
            v.tensor_reduce(S1[0:1, 513 + c2:514 + c2], PTF[0:1, 0:128], AX.X,
                            ALU.min if c2 == 0 else ALU.max)
        bc_tok = ks.tok("vector")
        t.wait_tok(bc_tok)
        for c2 in range(2):
            for pbase in range(2):
                t.matmul(PS[c2][pbase * 64:(pbase + 1) * 64, 0:1],
                         onesf[0:1, 0:64], S1[0:1, 513 + c2:514 + c2],
                         start=True, stop=True).then_inc(trs, 1)
        v.wait(trs)
        v.tensor_copy(lo, PS[0][:, 0:1])
        v.tensor_copy(hi, PS[1][:, 0:1])
        for it2 in range(44):
            v.tensor_add(mid, lo, hi)
            v.tensor_scalar(mid, mid, 0.5, None, ALU.mult)
            v.tensor_scalar(S2[:, 0:96], CKM, mid, None, ALU.is_ge)
            v.tensor_reduce(cnt, S2[:, 0:96], AX.X, ALU.add)
            ct_tok = ks.tok("vector")
            t.wait_tok(ct_tok)
            for pbase in range(2):
                t.matmul(PS[2][pbase * 64:(pbase + 1) * 64, 0:1],
                         onesf[:, 0:64],
                         cnt, start=True, stop=True).then_inc(trs, 1)
            v.wait(trs)
            v.tensor_copy(tot, PS[2][:, 0:1])
            v.tensor_scalar(cge, tot, 4096.0, None, ALU.is_ge)
            v.tensor_scalar(ncg, cge, -1.0, 1.0, ALU.mult, ALU.add)
            v.tensor_sub(S2[:, 100:101], mid, lo)
            v.scalar_tensor_tensor(lo, S2[:, 100:101], cge, lo, ALU.mult, ALU.add)
            v.tensor_sub(S2[:, 100:101], mid, hi)
            v.scalar_tensor_tensor(hi, S2[:, 100:101], ncg, hi, ALU.mult, ALU.add)
        v.tensor_scalar(S1[:, 0:96], CKM, lo, None, ALU.is_ge)
        v.tensor_mul(S2[:, 0:96], S1[:, 0:96], S1[:, 96:192])
        v.tensor_sub(S2[:, 0:96], S1[:, 0:96], S2[:, 0:96])
        v.tensor_reduce(cnt, S2[:, 0:96], AX.X, ALU.add)
        v.tensor_add(S1[:, 96:192], S1[:, 96:192], S2[:, 0:96])
        nv_tok = ks.tok("vector")
        t.wait_tok(nv_tok)
        t.matmul(PS[2][0:1, 0:1], cnt, onesf[:, 0:1],
                 start=True, stop=True).then_inc(trs, 1)
        v.wait(trs)
        v.tensor_copy(S1[0:1, 208:209], PS[2][0:1, 0:1])
        cs_tok = ks.tok("vector")
        s.wait_tok(cs_tok)
        s.dma_start(out=cc_in[it - 1:it], in_=S1[0:1, 208:209]).then_inc(cst, 16)
        g.wait(cst, cst.n)
        if it == 2:
            g.wait(ccrd, ccrd.n)
        g.collective_compute(
            "AllReduce", ALU.add,
            replica_groups=[list(range(NCORES))],
            ins=[cc_in[0:4].opt()], outs=[cc_out[0:4].opt()]).then_inc(ccs, 1)
        sy.wait(ccs, ccs.n)
        sy.dma_start(out=S1[0:1, 212:216], in_=cc_out[0:4]).then_inc(ccrd, 16)
        v.wait(ccrd, ccrd.n)
        v.wait(flst, flst.n)
        v.tensor_scalar(S1[0:1, 220:221], S1[0:1, 211 + it:212 + it],
                        1.0 / (NCORES * 4096.0), None, ALU.mult)
        v.tensor_scalar(S1[0:1, 221:222], S1[0:1, 220:221], 0.7, None, ALU.is_lt)
        fl_tok = ks.tok("vector")
        s.wait_tok(fl_tok)
        s.dma_start(out=flag_d[it - 1:it], in_=S1[0:1, 221:222]).then_inc(flst, 16)

    pass_m(1)
    s.wait(hst)
    ks.barrier()
    pass_p(1, xn1_d)
    topk(1)
    s.wait(xst)
    ks.barrier()
    pass_m(2)
    s.wait(hst)
    ks.barrier()
    pass_p(2, xn2_d)
    topk(2)
    s.wait(xst)
    ks.barrier()
    pass_m(3)
    s.wait(hst)
    ks.barrier()
    pass_p(3, None)
    ks.barrier()

    sy.wait(flst, flst.n)
    fll = ks.sem()
    sy.dma_start(out=S1[:, 230:231],
                 in_=bass.AP(flag_d.tensor, 0, [[0, 128], [1, 1]])).then_inc(fll, 16)
    sy.dma_start(out=S1[:, 231:232],
                 in_=bass.AP(flag_d.tensor, 1, [[0, 128], [1, 1]])).then_inc(fll, 16)
    v.wait(fll)
    xld2 = ks.sem()
    for cb in range(8):
        sy.wait_tok(ks.tok("vector"))
        sy.dma_start(out=B1[:, 0:2048],
                     in_=xn1_d[cb * 128:(cb + 1) * 128, :]).then_inc(xld2, 16)
        sy.dma_start(out=B1[:, 2048:4096],
                     in_=xn2_d[cb * 128:(cb + 1) * 128, :]).then_inc(xld2, 16)
        v.wait(xld2)
        v.tensor_sub(S2[:, 0:2048], B1[:, 2048:4096], B2[:, cb * 2048:(cb + 1) * 2048])
        v.scalar_tensor_tensor(S2[:, 0:2048], S2[:, 0:2048], S1[:, 231:232],
                               B2[:, cb * 2048:(cb + 1) * 2048], ALU.mult, ALU.add)
        v.tensor_sub(WST[:, 0:2048], B1[:, 0:2048], S2[:, 0:2048])
        v.scalar_tensor_tensor(S2[:, 0:2048], WST[:, 0:2048], S1[:, 230:231],
                               S2[:, 0:2048], ALU.mult, ALU.add)
        v.tensor_copy(B2[:, cb * 2048:(cb + 1) * 2048], S2[:, 0:2048])
    fst = ks.sem()
    fst_c = [fst.n]
    t.wait_tok(ks.tok("vector"))
    prev = None
    for tb in range(NT):
        for cb in range(8):
            if prev is not None:
                t.wait_tok(prev)
            t.transpose(PT[:, :], B2[:, cb * 2048 + tb * 128:cb * 2048 + tb * 128 + 128],
                        ident[:, :]).then_inc(trs, 1)
            v.wait(trs)
            v.tensor_copy(B1[:, cb * 128:(cb + 1) * 128], PT[:, :])
            prev = ks.tok("vector")
        v.wait(fst, fst_c[tb])
        v.tensor_copy(S2[:, 0:1024], B1[:, 0:1024])
        s.wait_tok(ks.tok("vector"))
        s.dma_start(out=tm_dram(f_dram)[:, tb, :], in_=S2[:, 0:1024]).then_inc(fst, 16)
        fst_c.append(fst.n)
    s.wait(fst)


def st_combine(ks, xsrc, s_dram, f_dram, xout):
    v, s, t, sy, g = (ks.r[e] for e in ("vector", "scalar", "tensor", "sync", "gpsimd"))
    A1, B1, C1 = ks.sb["A1"], ks.sb["B1"], ks.sb["C1"]
    Bf = B1.bitcast(F32)
    ldx, st = ks.sem(), ks.sem()
    sy.dma_start(out=a3d(A1), in_=tm_dram(xsrc)).then_inc(ldx, 16)
    for half in range(2):
        cols = slice(half * 8192, (half + 1) * 8192)
        ld = ks.sem()
        if half == 1:
            sy.wait_tok(ks.tok("vector"))
        sy.dma_start(out=a3d(Bf), in_=tm_dram(f_dram)[:, half * 8:(half + 1) * 8, :]).then_inc(ld, 16)
        sy.dma_start(out=a3d(C1), in_=tm_dram(s_dram)[:, half * 8:(half + 1) * 8, :]).then_inc(ld, 16)
        s.wait(ld)
        s.activation(Bf[:, :], Bf[:, :], AF.Silu)
        v.wait(ld)
        v.wait(ldx)
        v.wait_tok(ks.tok("scalar"))
        v.tensor_mul(Bf[:, :], Bf[:, :], C1[:, :])
        v.tensor_add(A1[:, cols], A1[:, cols], Bf[:, :])
        s.wait_tok(ks.tok("vector"))
        s.dma_start(out=tm_dram(xout)[:, half * 8:(half + 1) * 8, :],
                    in_=a3d(A1[:, cols])).then_inc(st, 16)
    s.wait(st)


def build_nc(cfg):
    import contextlib
    nc = bass.Bass("TRN2", target_bir_lowering=False)
    es = contextlib.ExitStack()
    nc._es_keepalive = es

    ins = {}
    ins["L0_attn_x"] = nc.dram_tensor("L0_attn_x", [T, D], F32, kind="ExternalInput").ap()

    def win(name, shape):
        ins[name] = nc.dram_tensor(name, list(shape), F32, kind="ExternalInput").ap()
    for spec_name, spec_shape in INPUT_SPECS:
        if spec_name != "L0_attn_x":
            win(spec_name, spec_shape)

    y = nc.dram_tensor("y", [T, D], F32, kind="ExternalOutput").ap()
    tap_outs = {}
    for tp in cfg.get("taps", ()):
        tap_outs[tp] = nc.dram_tensor("tap_" + tp, [T, D], F32, kind="ExternalOutput").ap()

    ident_in = nc.dram_tensor("c_ident", [128, 128], BF16, kind="ExternalInput").ap()
    identf_in = nc.dram_tensor("c_identf", [128, 128], F32, kind="ExternalInput").ap()
    ltri_in = nc.dram_tensor("c_ltri", [128, 128], BF16, kind="ExternalInput").ap()
    onesb_in = nc.dram_tensor("c_onesb", [128, 64], BF16, kind="ExternalInput").ap()
    onesf_in = nc.dram_tensor("c_onesf", [128, 64], F32, kind="ExternalInput").ap()
    cvec_in = nc.dram_tensor("c_vec", [16], F32, kind="ExternalInput").ap()
    cosd = nc.dram_tensor("rope_cos", [T, 32], F32, kind="ExternalInput").ap()
    sind = nc.dram_tensor("rope_sin", [T, 32], F32, kind="ExternalInput").ap()
    ltri8_in = nc.dram_tensor("c_ltri8", [128, 128], F32, kind="ExternalInput").ap()
    blk8_in = nc.dram_tensor("c_blk8", [128, 128], F32, kind="ExternalInput").ap()
    m4s_in = nc.dram_tensor("c_m4s", [128, 128], F32, kind="ExternalInput").ap()
    bmaskT_in = nc.dram_tensor("c_bmaskT", [128, 128], BF16, kind="ExternalInput").ap()
    rm16_in = nc.dram_tensor("c_rm16", [128, 16], F32, kind="ExternalInput").ap()

    ks = KS(nc, es)
    ks.dram["idx_d"] = nc.dram_tensor("idx_d", [4097, 1], I32, kind="Internal").ap()
    ks.dram["gat_d"] = nc.dram_tensor("gat_d", [4097, 1], F32, kind="Internal").ap()
    ks.dram["f_ext"] = nc.dram_tensor("f_ext", [T + 1, D], F32, kind="Internal").ap()
    ks.dram["rope_cos"] = cosd
    ks.dram["rope_sin"] = sind
    for nm, shape, dt in (
        ("xn", [T, D], F32), ("xnT", [D, T], BF16), ("xcur", [T, D], F32),
        ("s_raw", [T, D], F32), ("f_raw", [T, D], F32),
        ("scr_q", [T, D], BF16), ("scr_k", [T, D], BF16), ("scr_v", [T, D], BF16),
        ("pbfwd", [16, 4096], BF16),
        ("xn_bf", [T + 1, D], BF16),
        ("wg_bf", [8 * 4096, D], BF16), ("wu_bf", [8 * 4096, D], BF16),
        ("wd_bf", [8 * D, 4096], BF16),
        ("thr", [8], F32),
        ("qkvw_bf", [3072, D], BF16), ("projw_bf", [D, D], BF16),
        ("h_scr", [T, 4096], BF16),
        ("lw_d", [T, D], F32), ("g_d", [T, D], F32), ("kh_d", [T, D], BF16),
        ("rtT_d", [D, T], BF16), ("ktT_d", [D, T], BF16), ("roT_d", [D, T], BF16),
        ("pcT_d", [1024, 256], F32), ("x1_d", [T, D], F32),
        ("wmT_d", [48 * 128, 2048], BF16), ("wpT_d", [96 * 128, 1024], BF16),
        ("hT_d", [96 * 128, 2048], BF16),
        ("xn1_d", [8 * 128, 2048], BF16), ("xn2_d", [8 * 128, 2048], BF16),
        ("wc1_bf", [256, 1024], BF16),
        ("cc_in", [4], F32), ("cc_out", [4], F32), ("flag_d", [4], F32),
    ):
        ks.dram_scratch(nm, shape, dt)
    ks.ncores = cfg.get("ncores", 8)
    ks.l0dump = cfg.get("l0dump", False)
    if ks.l0dump:
        ks.dram_scratch("dump_d", [128, 16384], BF16)

    ks.sbuf("A1", [128, 16384], F32)
    ks.sbuf("B1", [128, 16384], BF16)
    ks.sbuf("B2", [128, 16384], BF16)
    ks.sbuf("C1", [128, 8192], F32)
    ks.sbuf("CB", [128, 8192], BF16)
    ks.sbuf("S1", [128, 2048], F32)
    ks.sbuf("S2", [128, 2048], F32)
    ks.sbuf("WST", [128, 2048], F32)
    ks.sbuf("EXT", [128, 2432], BF16)
    ks.sbuf("MI", [128, 256], I32)
    ks.sbuf("ident", [128, 128], BF16)
    ks.sbuf("identf", [128, 128], F32)
    ks.sbuf("ltri", [128, 128], BF16)
    ks.sbuf("onesb", [128, 64], BF16)
    ks.sbuf("onesf", [128, 64], F32)
    ks.sbuf("cvec", [128, 16], F32)
    ks.cin = {"ltri8": ltri8_in, "blk8": blk8_in, "m4s": m4s_in, "bmaskT": bmaskT_in,
              "rm16": rm16_in}
    for pn in ("PS0", "PS1", "PS2", "PS3", "PR", "PQ", "PE1"):
        ks.psum(pn, [128, 512], F32)
    ks.psum("PTF", [128, 512], F32)
    ks.sb["PT"] = ks.sb["PTF"].bitcast(BF16)[:, 0:128]

    ld0 = ks.sem()
    ks.r["sync"].dma_start(out=ks.sb["ident"][:, :], in_=ident_in[:, :]).then_inc(ld0, 16)
    ks.r["sync"].dma_start(out=ks.sb["identf"][:, :], in_=identf_in[:, :]).then_inc(ld0, 16)
    ks.r["sync"].dma_start(out=ks.sb["ltri"][:, :], in_=ltri_in[:, :]).then_inc(ld0, 16)
    ks.r["sync"].dma_start(out=ks.sb["onesb"][:, :], in_=onesb_in[:, :]).then_inc(ld0, 16)
    ks.r["sync"].dma_start(out=ks.sb["onesf"][:, :], in_=onesf_in[:, :]).then_inc(ld0, 16)
    ks.r["sync"].dma_start(out=ks.sb["cvec"][:, :], in_=bcast_dram_row(cvec_in, 16)).then_inc(ld0, 16)
    for e in ENGINES:
        ks.r[e].wait(ld0)

    layers = cfg.get("layers", [0, 1, 2, 3])
    upto = cfg.get("upto", "combine")
    xsrc = ins["L0_attn_x"]
    for li, lay in enumerate(layers):
        last = li == len(layers) - 1
        st_ln(ks, xsrc, ks.dram["xn"], ks.dram["xnT"])
        ks.barrier()
        if upto in ("attn", "ffn", "combine"):
            if lay == 3:
                Wa = {k: ins["L3_attn_" + k] for k in
                      ("Wq", "Wk", "Wv", "Wo", "bq", "bk", "bv", "bo")}
                st_l3_attn(ks, Wa, ks.dram["xnT"], ks.dram["s_raw"])
            elif lay == 2:
                Wa = {k: ins["L2_attn_" + k] for k in
                      ("Wq", "Wk", "Wv", "Wo", "bo", "rel_pos_bias")}
                st_l2_attn(ks, Wa, ks.dram["xnT"], ks.dram["s_raw"])
            elif lay == 1:
                Wa = {k: ins["L1_attn_" + k] for k in
                      ("qkv_w", "qkv_b", "proj_w", "proj_b", "dwc_w", "dwc_b")}
                st_l1_attn(ks, Wa, ks.dram["xnT"], ks.dram["s_raw"])
            elif lay == 0:
                Wa = {k: ins["L0_attn_" + k] for k in
                      ("ln1_w", "mu_r", "mu_k", "mu_v", "mu_w", "mu_g",
                       "Wr", "Wk", "Wv", "Wg", "Wo", "w0", "wA", "wB", "u",
                       "gn_w", "gn_b", "ln2_w", "ffn_w1", "ffn_w2")}
                st_l0_attn(ks, Wa, ks.dram["xn"], ks.dram["xnT"], ks.dram["s_raw"])
            ks.barrier()
        if upto in ("ffn", "combine"):
            if lay == 3:
                Wf = {"weight": ins["L3_ffn_weight"], "bias": ins["L3_ffn_bias"]}
                st_l3_ffn(ks, Wf, ks.dram["xn"], ks.dram["f_raw"])
            elif lay == 2:
                Wf = {k: ins["L2_ffn_" + k] for k in
                      ("gate_w", "w_gate", "w_up", "w_down")}
                st_l2_ffn(ks, Wf, ks.dram["xn"], ks.dram["xnT"], ks.dram["f_ext"])
            elif lay == 1:
                Wf = {k: ins["L1_ffn_" + k] for k in
                      ("m1", "m2", "mb", "w1", "w2", "b2")}
                st_l1_ffn(ks, Wf, ks.dram["xnT"], ks.dram["f_raw"])
            elif lay == 0:
                Wf = {k: ins["L0_ffn_" + k] for k in ("Wm", "Wp", "Wc1", "Wc2")}
                st_l0_ffn(ks, Wf, ks.dram["xnT"], ks.dram["f_raw"])
            ks.barrier()
        if upto == "combine":
            fd = ks.dram["f_ext"][0:T, :] if lay == 2 else ks.dram["f_raw"]
            st_combine(ks, xsrc, ks.dram["s_raw"], fd,
                       y if last else ks.dram["xcur"])
            ks.barrier()
        xsrc = ks.dram["xcur"]

    sbtaps = cfg.get("sbtaps", ())
    if tap_outs or sbtaps:
        tsem = ks.sem()
        for tp, out_ap in tap_outs.items():
            src_ap = ks.dram[tp]
            if tp == "f_raw" and layers and layers[-1] == 2:
                src_ap = ks.dram["f_ext"][0:T, :]
            ks.r["sync"].dma_start(out=out_ap[:, :],
                                   in_=src_ap[:, :]).then_inc(tsem, 16)
        for nm, cols in sbtaps:
            oap = nc.dram_tensor("sbtap_" + nm, [128, cols], F32,
                                 kind="ExternalOutput").ap()
            ks.r["sync"].dma_start(out=oap[:, :],
                                   in_=ks.sb[nm][:, 0:cols]).then_inc(tsem, 16)
        ks.r["sync"].wait(tsem)

    with nc.Block() as block:
        for e in ENGINES:
            rec = ks.r[e]
            getattr(block, e)(lambda eng, rec=rec: rec.replay(eng))
    es.close()
    return nc


def make_consts():
    ident = np.eye(128, dtype=ml_dtypes.bfloat16)
    inv_freq = (1.0 / 10000.0 ** (np.arange(0, DK, 2, dtype=np.float32) / DK)).astype(np.float32)
    tt = np.arange(T, dtype=np.float32)
    freqs = tt[:, None] * inv_freq[None, :]
    midx = (2 * np.arange(32)) % 32
    ang = freqs[:, midx]
    cvec = np.zeros(16, np.float32)
    cvec[0] = 0.5
    ltri = np.triu(np.ones((128, 128)), 1).astype(ml_dtypes.bfloat16)
    ii = np.arange(128)
    same8 = (ii[:, None] // 8) == (ii[None, :] // 8)
    ltri8 = (same8 & (ii[:, None] < ii[None, :])).astype(np.float32)
    blk8 = same8.astype(np.float32)
    m4s = (ii[:, None] == ((ii[None, :] // 8) * 8 + 4)).astype(np.float32)
    bmaskT = (same8 & (ii[:, None] < ii[None, :])).astype(ml_dtypes.bfloat16)
    rm16 = (ii[:, None] // 8 == np.arange(16)[None, :]).astype(np.float32)
    return {"c_ident": ident,
            "c_rm16": rm16,
            "c_ltri8": ltri8,
            "c_blk8": blk8,
            "c_m4s": m4s,
            "c_bmaskT": bmaskT,
            "c_identf": np.eye(128, dtype=np.float32),
            "c_ltri": ltri,
            "c_onesb": np.ones((128, 64), ml_dtypes.bfloat16),
            "c_onesf": np.ones((128, 64), np.float32),
            "c_vec": cvec,
            "rope_cos": np.cos(ang).astype(np.float32),
            "rope_sin": np.sin(ang).astype(np.float32)}


_NC_CACHE = {}


def kernel(**inputs):
    from concourse.bass_utils import run_bass_kernel_spmd
    xs = np.asarray(inputs["L0_attn_x"], np.float32)
    consts = make_consts()
    if "full" not in _NC_CACHE:
        _NC_CACHE["full"] = build_nc({"layers": [0, 1, 2, 3], "upto": "combine",
                                      "ncores": 8, "taps": ()})
    nc = _NC_CACHE["full"]
    per_core = []
    for c in range(8):
        m = {"L0_attn_x": np.ascontiguousarray(xs[c])}
        for name, _shape in INPUT_SPECS:
            m[name] = np.asarray(inputs[name], np.float32)
        m.update(consts)
        per_core.append(m)
    res = run_bass_kernel_spmd(nc, per_core, core_ids=list(range(8)))
    return np.stack([np.asarray(res.results[c]["y"], np.float32) for c in range(8)])
